# version 01
# baseline (speedup 1.0000x reference)
import os
import numpy as np
import ml_dtypes
import concourse.bass as bass
import concourse.mybir as mybir
from concourse.bass_utils import run_bass_kernel_spmd

F32 = mybir.dt.float32
BF16 = mybir.dt.bfloat16
F16 = mybir.dt.float16
I32 = mybir.dt.int32
AF = mybir.ActivationFunctionType
ALU = mybir.AluOpType
AXX = mybir.AxisListType.X

T = 2048
D = 1024
NT = 16
NCT = 8
H = 16
HD = 64
DH = 4096

ENGINES = ("sy", "sc", "ve", "gp", "te")


class Buf:
    __slots__ = ("name", "last_write", "readers", "dram")

    def __init__(self, name, dram=False):
        self.name = name
        self.last_write = None
        self.readers = []
        self.dram = dram


class Prog:

    def __init__(self, nc):
        self.nc = nc
        self.steps = []
        self.sems = {}
        self.sem_cms = []
        for e in ENGINES:
            self._mk_sem(e + "c")
        for e in ("sy", "sc", "gp", "ve"):
            self._mk_sem(e + "d")
        self.barriers = []
        self.serial = bool(os.environ.get("BK_SERIAL"))
        self.flush_emit = None
        self.flush_buf = None
        self.recent_ve = []

    def _mk_sem(self, name):
        cm = self.nc.semaphore(name)
        h = cm.__enter__()
        self.sems[name] = h
        self.sem_cms.append(cm)

    def op(self, eng, emit, reads=(), writes=(), dma=False, tag="", slack=0,
           _noflush=False):
        if (eng == "ve" and not _noflush and self.flush_emit is not None):
            hot = False
            for b in reads:
                lw = b.last_write
                if lw is not None and lw[1] in self.recent_ve:
                    hot = True
                    break
            if hot:
                self.op("ve", self.flush_emit, writes=(self.flush_buf,),
                        tag="autoflush", _noflush=True)
        kind = "dma" if dma else "cmp"
        sem_id = eng + ("d" if dma else "c")
        idx = len(self.steps)
        deps = set()

        def add(dep, suppressible):
            if dep is None:
                return
            dsem, didx = dep
            if dsem == eng + "c" and kind == "cmp":
                return
            if dma and suppressible and dsem == sem_id:
                return
            deps.add(didx)

        for b in reads:
            add(b.last_write, not b.dram)
        for b in writes:
            add(b.last_write, not b.dram)
            for r in b.readers:
                add(r, not b.dram)
        if self.serial and idx > 0:
            deps.add(idx - 1)
        self.steps.append([eng, kind, emit, deps, tag, slack])
        if eng == "ve" and not dma:
            self.recent_ve.append(idx)
            if len(self.recent_ve) > 2:
                self.recent_ve.pop(0)
        dep = (sem_id, idx)
        for b in writes:
            b.last_write = dep
            b.readers = []
        for b in reads:
            b.readers.append(dep)
        return idx

    def barrier(self):
        self.barriers.append(len(self.steps))

    def finalize_emitters(self):
        n = len(self.steps)
        is_target = [st[1] == "dma" for st in self.steps]
        for st in self.steps:
            for d in st[3]:
                is_target[d] = True
        barrier_waits = []
        for bpos in self.barriers:
            last = {}
            for i in range(bpos):
                st = self.steps[i]
                last[(st[0], st[1])] = i
            for i in last.values():
                is_target[i] = True
            barrier_waits.append(set(last.values()))
        final = {}
        for i, st in enumerate(self.steps):
            final[(st[0], st[1])] = i
        for i in final.values():
            is_target[i] = True
        self.final_anchor = set(final.values())
        self.inc_val = [0] * n
        counts = {s: 0 for s in self.sems}
        for i, st in enumerate(self.steps):
            if is_target[i]:
                sem_id = st[0] + ("d" if st[1] == "dma" else "c")
                counts[sem_id] += 16 if st[1] == "dma" else 1
                self.inc_val[i] = counts[sem_id]
        self.is_target = is_target
        know = {e: {} for e in ENGINES}
        step_know = [None] * n
        self.step_waits = [None] * n
        bi = 0
        pending = {e: set() for e in ENGINES}
        for i, st in enumerate(self.steps):
            eng, kind = st[0], st[1]
            while bi < len(self.barriers) and self.barriers[bi] <= i:
                for e in ENGINES:
                    pending[e] |= barrier_waits[bi]
                bi += 1
            deps = set(st[3]) | pending[eng]
            pending[eng] = set()
            need = {}
            for d in deps:
                dst = self.steps[d]
                dsem = dst[0] + ("d" if dst[1] == "dma" else "c")
                v = self.inc_val[d]
                assert v > 0, f"dep on non-target step {d} ({dst[4]})"
                if dsem == eng + "c" and kind == "cmp":
                    continue
                if need.get(dsem, 0) < v:
                    need[dsem] = v
            kn = know[eng]
            waits = {s: v for s, v in need.items() if kn.get(s, 0) < v}
            self.step_waits[i] = waits
            kn2 = dict(kn)
            for d in deps:
                dst = self.steps[d]
                dsem = dst[0] + ("d" if dst[1] == "dma" else "c")
                v = self.inc_val[d]
                if kn2.get(dsem, 0) < v:
                    kn2[dsem] = v
                pk = step_know[d]
                if pk:
                    for s2, v2 in pk.items():
                        if kn2.get(s2, 0) < v2:
                            kn2[s2] = v2
            sem_id = eng + ("d" if kind == "dma" else "c")
            if kind == "cmp":
                if self.inc_val[i] and kn2.get(sem_id, 0) < self.inc_val[i]:
                    kn2[sem_id] = self.inc_val[i]
                know[eng] = kn2
                step_know[i] = kn2
            else:
                know[eng] = kn2
                pk = dict(kn2)
                if self.inc_val[i]:
                    pk[sem_id] = max(pk.get(sem_id, 0), self.inc_val[i])
                step_know[i] = pk
        self.final_counts = counts

        per_eng = {e: [] for e in ENGINES}
        for i, st in enumerate(self.steps):
            per_eng[st[0]].append(i)

        def emit_for(eng, e):
            for i in per_eng[eng]:
                st = self.steps[i]
                for s, v in self.step_waits[i].items():
                    e.wait_ge(self.sems[s], v)
                if st[5]:
                    e.nop(cycle_cnt=st[5], nofuse=True)
                inst = st[2](e)
                if self.is_target[i]:
                    inst.then_inc(self.sems[st[0] + ("d" if st[1] == "dma" else "c")],
                                  16 if st[1] == "dma" else 1)

        return emit_for

    def final_waits(self, e):
        for s, v in self.final_counts.items():
            if v:
                e.wait_ge(self.sems[s], v)

    def close_sems(self):
        for cm in self.sem_cms:
            cm.__exit__(None, None, None)



class Ctx:

    def __init__(self, nc, upto, debug=False):
        self.nc = nc
        self.P = Prog(nc)
        self.upto = upto
        self.arena = None
        self.debug = debug
        self.dbg_names = []

    def flush_ve(self, buf, b):
        self.P.op("ve", (lambda bb=buf: (lambda e: e.memset(bb, 0.0)))(),
                  writes=(b,), tag="veflush")
        self.P.op("ve", (lambda bb=buf: (lambda e: e.memset(bb, 0.0)))(),
                  writes=(b,), tag="veflush2")

    def dump(self, name, ap, reads):
        if not self.debug:
            return
        d = self.nc.dram_tensor("dbg_" + name, list(ap.shape), ap.dtype, kind="ExternalOutput").ap()
        self.dbg_names.append("dbg_" + name)
        self.P.op("sy", (lambda a=ap, dd=d: (lambda e: e.dma_start(out=dd, in_=a)))(),
                  reads=tuple(reads), writes=(Buf("dbg_" + name, dram=True),), dma=True, tag="dump" + name)

    def arena_reserve(self, nbytes=None):
        if nbytes is None:
            nbytes = (self.nc.sbuf_bytes_remaining - 64) & ~31
        self._arena_bytes = nbytes
        h = self.nc.alloc_sbuf_tensor("arena_reserve", [128, nbytes], mybir.dt.uint8)
        self._arena_base = self.nc.lookup_mloc(h).addr
        self._arena_id = 0

    def arena_open(self):
        self._cursor = 0
        self._arena_id += 1

    def at(self, name, shape, dtype):
        per = 1
        for d in shape[1:]:
            per *= d
        per *= mybir.dt.size(dtype)
        off = self._arena_base + self._cursor
        self._cursor = (self._cursor + per + 31) & ~31
        assert self._cursor <= self._arena_bytes, f"arena overflow at {name}: {self._cursor}"
        h = self.nc.alloc_sbuf_tensor_at(f"{name}_a{self._arena_id}", shape, dtype, offset=off)
        return h.ap()

    def arena_alloc(self):
        pass

    def arena_mark(self):
        return self._cursor

    def arena_rewind(self, mark):
        self._cursor = mark
        self.P.barrier()

    def arena_close(self):
        self.P.barrier()


def ln_finish_and_apply(C, st, cols, b_st, b_cols, b_xtm, x_tm, name):
    P = C.P
    P.op("ve", lambda e: e.memset(cols[:, :, 2], 0.0), writes=(b_cols,), tag=f"{name}_flush")
    P.op("ve", lambda e: e.memset(cols[:, :, 2], 0.0), writes=(b_cols,), tag=f"{name}_flush2")
    P.op("ve", lambda e: e.tensor_scalar(cols[:, :, 1], st[:, :, 1], 1.0 / D, None, ALU.mult),
         reads=(b_st,), writes=(b_cols,), tag=f"{name}_mean", slack=768)
    P.op("ve", lambda e: e.tensor_scalar(cols[:, :, 0], st[:, :, 0], 1.0 / D, 1e-5, ALU.mult, ALU.add),
         reads=(b_st,), writes=(b_cols,), tag=f"{name}_ex2", slack=256)
    P.op("ve", lambda e: e.tensor_tensor(cols[:, :, 2], cols[:, :, 1], cols[:, :, 1], ALU.mult),
         reads=(b_cols,), writes=(b_cols,), tag=f"{name}_m2", slack=256)
    P.op("ve", lambda e: e.tensor_tensor(cols[:, :, 0], cols[:, :, 0], cols[:, :, 2], ALU.subtract),
         reads=(b_cols,), writes=(b_cols,), tag=f"{name}_var", slack=256)
    P.op("sc", lambda e: e.activation(cols[:, :, 2], cols[:, :, 0], AF.Sqrt),
         reads=(b_cols,), writes=(b_cols,), tag=f"{name}_sqrt", slack=768)
    P.op("ve", lambda e: e.reciprocal(cols[:, :, 0], cols[:, :, 2]),
         reads=(b_cols,), writes=(b_cols,), tag=f"{name}_rcp", slack=768)
    P.op("ve", lambda e: e.tensor_tensor(cols[:, :, 1], cols[:, :, 1], cols[:, :, 0], ALU.mult),
         reads=(b_cols,), writes=(b_cols,), tag=f"{name}_mr", slack=256)
    for tt in range(NT):
        P.op("ve", (lambda t=tt: (lambda e: e.tensor_scalar(
            x_tm[:, t, :], x_tm[:, t, :], cols[:, t, 0:1], cols[:, t, 1:2],
            ALU.mult, ALU.subtract)))(),
            reads=(b_cols, b_xtm[tt]), writes=(b_xtm[tt],), tag=f"{name}_ap{tt}", slack=(512 if tt < 2 else 0))



def build_l0_attn(C):
    nc, P = C.nc, C.P
    x_tm, xfm = C.x_tm, C.xfm
    b_xtm, b_xfm = C.b_xtm, C.b_xfm
    QC = 256
    NQC = T // QC

    C.arena_open()
    kfm = C.at("kfm", [128, NCT, T], BF16)
    vtm = C.at("vtm", [128, NT, H * (HD + 1)], BF16)
    qfm = C.at("qfm", [128, NCT, QC], BF16)
    expT = C.at("expT", [128, 2048], BF16)
    ostr = C.at("ostr", [128, NCT, QC], BF16)
    otmp = C.at("otmp", [64, QC], BF16)
    wbuf = C.at("wbuf", [128, NCT, D], BF16)
    mcol = C.at("mcol", [128, H], F32)
    qsq = C.at("qsq", [128, 256], BF16)
    h2d = C.at("h2d", [2, 24], F32)
    zrow = C.at("zrow", [1, 512], F32)
    zbc = C.at("zbc", [128, 512], F32)
    lst = C.at("l0a_st", [128, NT, 2], F32)
    lcols = C.at("l0a_cols", [128, NT, 3], F32)
    lzsq = C.at("l0a_zsq", [128, D], BF16)
    C.arena_alloc()

    b_kfm = Buf("kfm"); b_vtm = Buf("vtm"); b_qfm = Buf("qfm")
    b_expT = Buf("expT"); b_ostr = Buf("ostr"); b_wbuf = Buf("wbuf"); b_otmp = Buf("otmp")
    b_mcol = Buf("mcol"); b_qsq = Buf("qsq")
    b_h2d = Buf("h2d"); b_zrow = Buf("zrow"); b_zbc = Buf("zbc")
    b_lst = Buf("lst"); b_lcols = Buf("lcols"); b_lzsq = Buf("lzsq")

    pS, pO, pC2 = C.pS, C.pO, C.pC
    bS = C.bS
    bO, bC = C.bO, C.bC

    sc_mrow = C.sc_mrow
    b_scmd = C.b_scmd

    def load_wT(key):
        src = C.sc_w[key]
        for kt in range(NCT):
            P.op("sy", (lambda k=kt, s=src: (lambda e: e.dma_start_transpose(
                out=wbuf[:, k, :], in_=s[:, 128 * k:128 * (k + 1)])))(),
                reads=(C.b_scw[key],), writes=(b_wbuf,), dma=True, tag=f"ldT_{key}{kt}")

    load_wT("wqT" if False else "wkT")
    for j in range(NCT):
        for n in range(4):
            q = n % 4
            for kt in range(NCT):
                P.op("te", (lambda jj=j, nn=n, k=kt, qq=q: (lambda e: e.matmul(
                    pS[:, 512 * qq:512 * (qq + 1)],
                    wbuf[:, k, 128 * jj:128 * (jj + 1)],
                    xfm[:, k, 512 * nn:512 * (nn + 1)],
                    start=(k == 0), stop=(k == NCT - 1))))(),
                    reads=(b_wbuf, b_xfm[kt]), writes=(bS[q],), tag=f"kproj{j}_{n}")
            P.op("ve", (lambda jj=j, nn=n, qq=q: (lambda e: e.tensor_copy(
                kfm[:, jj, 512 * nn:512 * (nn + 1)], pS[:, 512 * qq:512 * (qq + 1)])))(),
                reads=(bS[q],), writes=(b_kfm,), tag=f"kev{j}_{n}")

    P.op("ve", lambda e: e.memset(vtm, 1.0), writes=(b_vtm,), tag="vones")
    load_wT("wvT")
    for tt in range(NT):
        for half in range(2):
            q = half
            for kt in range(NCT):
                P.op("te", (lambda t=tt, hf=half, k=kt, qq=q: (lambda e: e.matmul(
                    pS[:, 512 * qq:512 * (qq + 1)],
                    xfm[:, k, 128 * t:128 * (t + 1)],
                    wbuf[:, k, 512 * hf:512 * (hf + 1)],
                    start=(k == 0), stop=(k == NCT - 1))))(),
                    reads=(b_wbuf, b_xfm[kt]), writes=(bS[q],), tag=f"vproj{tt}_{half}")
        vv = vtm[:, tt, :].rearrange("p (h e) -> p h e", e=HD + 1)
        P.op("ve", (lambda t=tt, out=vv: (lambda e: e.tensor_copy(
            out[:, 0:8, 0:64], pS[:, 0:512].rearrange("p (h e) -> p h e", e=64))))(),
            reads=(bS[0],), writes=(b_vtm,), tag=f"vev{tt}a")
        P.op("ve", (lambda t=tt, out=vv: (lambda e: e.tensor_copy(
            out[:, 8:16, 0:64], pS[:, 512:1024].rearrange("p (h e) -> p h e", e=64))))(),
            reads=(bS[1],), writes=(b_vtm,), tag=f"vev{tt}b")

    C.dump("kfm", kfm, (b_kfm,))
    C.dump("vtm", vtm, (b_vtm,))
    for j in (() if os.environ.get("BK_NONORM") else range(NCT)):
        P.op("sc", (lambda jj=j: (lambda e: e.activation(expT, kfm[:, jj, :], AF.Square)))(),
             reads=(b_kfm,), writes=(b_expT,), tag=f"ksq{j}")
        for n in range(4):
            P.op("te", (lambda nn=n: (lambda e: e.matmul(
                pS[0:2, 512 * nn:512 * (nn + 1)], C.s_e2,
                expT[:, 512 * nn:512 * (nn + 1)],
                start=True, stop=True)))(),
                reads=(b_expT, C.b_ones), writes=(bS[n],), tag=f"knrm{j}_{n}")
        P.op("ve", (lambda jj=j: (lambda e: e.reduce_max(
            h2d[:, jj:jj + 1], pS[0:2, :], AXX)))(),
            reads=(bS[0], bS[1], bS[2], bS[3]), writes=(b_h2d,), tag=f"knmx{j}")
        C.flush_ve(qsq[:, 0:64], b_qsq)

    for qc in range(NQC):
        load_wT("wqT")
        for j in range(NCT):
            q = j % 2
            for kt in range(NCT):
                P.op("te", (lambda jj=j, k=kt, c=qc, qq=q: (lambda e: e.matmul(
                    pS[:, 512 * qq:512 * qq + QC],
                    wbuf[:, k, 128 * jj:128 * (jj + 1)],
                    xfm[:, k, QC * c:QC * (c + 1)],
                    start=(k == 0), stop=(k == NCT - 1))))(),
                    reads=(b_wbuf, b_xfm[kt]), writes=(bS[q],), tag=f"qproj{qc}_{j}")
            P.op("ve", (lambda jj=j, qq=q: (lambda e: e.tensor_copy(
                qfm[:, jj, :], pS[:, 512 * qq:512 * qq + QC])))(),
                reads=(bS[q],), writes=(b_qfm,), tag=f"qev{qc}_{j}")
        if not os.environ.get("BK_NONORM"):
            for j in range(NCT):
                P.op("sc", (lambda jj=j: (lambda e: e.activation(qsq, qfm[:, jj, :], AF.Square)))(),
                     reads=(b_qfm,), writes=(b_qsq,), tag=f"qsq{qc}_{j}")
                P.op("te", (lambda jj=j: (lambda e: e.matmul(
                    pS[0:2, 256 * jj:256 * (jj + 1)], C.s_e2, qsq,
                    start=True, stop=True)))(),
                    reads=(b_qsq, C.b_ones), writes=(bS[j // 2],), tag=f"qnrm{qc}_{j}")
            P.op("ve", lambda e: e.reduce_max(
                h2d[:, 8:16].rearrange("a (h o) -> a h o", o=1),
                pS[0:2, :].rearrange("a (h n) -> a h n", n=256), AXX),
                reads=(bS[0], bS[1], bS[2], bS[3]), writes=(b_h2d,), tag=f"qnmx{qc}")
            C.flush_ve(qsq[:, 0:64], b_qsq)
        if os.environ.get("BK_NONORM"):
            P.op("ve", lambda e: e.memset(mcol, 0.0), writes=(b_mcol,), tag=f"mz{qc}")
        else:
            P.op("ve", lambda e: e.tensor_tensor(h2d[:, 16:24], h2d[:, 0:8], h2d[:, 8:16], ALU.mult),
                 reads=(b_h2d,), writes=(b_h2d,), tag=f"mm{qc}", slack=768)
            if C.debug and qc == 0:
                C.dump("h2d_post_mm", h2d, (b_h2d,))
            P.op("sc", lambda e: e.activation(h2d[:, 16:24], h2d[:, 16:24], AF.Sqrt),
                 reads=(b_h2d,), writes=(b_h2d,), tag=f"msq{qc}", slack=768)
            if C.debug and qc == 0:
                C.dump("h2d_post_sq", h2d, (b_h2d,))
            P.op("ve", lambda e: e.tensor_scalar(h2d[:, 16:24], h2d[:, 16:24], -0.125, None, ALU.mult),
                 reads=(b_h2d,), writes=(b_h2d,), tag=f"mneg{qc}", slack=768)
            for par in range(2):
                P.op("te", (lambda pp=par: (lambda e: e.matmul(
                    pC2[:, 8 * pp:8 * (pp + 1)], C.s_sel[:, 128 * pp:128 * (pp + 1)], h2d[:, 16:24],
                    start=(pp == 0), stop=(pp == 1))))(),
                    reads=(b_h2d, C.b_ones), writes=(bC,), tag=f"mbc{qc}_{par}", slack=768)
            for par in range(2):
                P.op("ve", (lambda pp=par: (lambda e: e.tensor_copy(
                    mcol.rearrange("p (a b) -> p a b", b=2)[:, :, pp],
                    pC2[:, 8 * pp:8 * (pp + 1)])))(),
                    reads=(bC,), writes=(b_mcol,), tag=f"mcol{qc}_{par}", slack=256)

        if qc == 0:
            C.dump("qfm", qfm, (b_qfm,))
        C.dump(f"h2d{qc}", h2d, (b_h2d,))
        C.dump(f"mcol{qc}", mcol, (b_mcol,))
        for h in range(H):
            j, lo = h // 2, 64 * (h % 2)
            for g in range(2):
                for ki in range(8):
                    kt = 8 * g + ki
                    q = ki // 2
                    c0 = 256 * (ki % 2) if QC == 256 else 0
                    P.op("te", (lambda jj=j, l=lo, k=kt, qq=q, cc=c0: (lambda e: e.matmul(
                        pS[:, 512 * qq + cc:512 * qq + cc + QC],
                        kfm[l:l + 64, jj, 128 * k:128 * (k + 1)],
                        qfm[l:l + 64, jj, :],
                        start=True, stop=True)))(),
                        reads=(b_kfm, b_qfm), writes=(bS[q],), tag=f"qk{qc}_{h}_{kt}")
                P.op("sc", (lambda hh=h: (lambda e: e.activation(
                    expT, pS, AF.Exp, bias=mcol[:, hh:hh + 1], scale=0.125)))(),
                    reads=(bS[0], bS[1], bS[2], bS[3], b_mcol), writes=(b_expT,),
                    tag=f"exp{qc}_{h}_{g}", slack=(512 if h == 0 and g == 0 else 0))
                for ki in range(8):
                    kt = 8 * g + ki
                    P.op("te", (lambda hh=h, k=kt, i=ki, gg=g: (lambda e: e.matmul(
                        pO[0:65, 0:QC],
                        vtm[:, k, 65 * hh:65 * hh + 65],
                        expT[:, 256 * i:256 * i + QC],
                        start=(k == 0), stop=(k == 15))))(),
                        reads=(b_vtm, b_expT), writes=(bO,), tag=f"pv{qc}_{h}_{kt}")
            if qc == 0 and h == 0:
                C.dump("expT", expT, (b_expT,))
            P.op("ve", lambda e: e.reciprocal(zrow[:, 0:QC], pO[64:65, 0:QC]),
                 reads=(bO,), writes=(b_zrow,), tag=f"zr{qc}_{h}")
            P.op("te", lambda e: e.matmul(pC2[:, 0:QC], C.s_ones_f32, zrow[:, 0:QC], start=True, stop=True),
                 reads=(b_zrow, C.b_ones), writes=(bC,), tag=f"zbc{qc}_{h}")
            P.op("ve", lambda e: e.tensor_copy(zbc[:, 0:QC], pC2[:, 0:QC]),
                 reads=(bC,), writes=(b_zbc,), tag=f"zbe{qc}_{h}")
            if h % 2 == 0:
                P.op("ve", (lambda jj=j: (lambda e: e.tensor_tensor(
                    ostr[0:64, jj, :], pO[0:64, 0:QC], zbc[0:64, 0:QC], ALU.mult)))(),
                    reads=(bO, b_zbc), writes=(b_ostr,), tag=f"oev{qc}_{h}")
            else:
                P.op("ve", lambda e: e.tensor_tensor(
                    otmp, pO[0:64, 0:QC], zbc[0:64, 0:QC], ALU.mult),
                    reads=(bO, b_zbc), writes=(b_otmp,), tag=f"oev{qc}_{h}")
                P.op("te", lambda e: e.matmul(
                    C.pD[64:128, 0:QC], C.s_id64, otmp, start=True, stop=True,
                    tile_position=(0, 64)),
                    reads=(b_otmp, C.b_ones), writes=(C.bD,), tag=f"osh{qc}_{h}")
                P.op("ve", (lambda jj=j: (lambda e: e.tensor_copy(
                    ostr[64:128, jj, :], C.pD[64:128, 0:QC])))(),
                    reads=(C.bD,), writes=(b_ostr,), tag=f"oevs{qc}_{h}")

        load_wT("woT")
        for ttl in range(QC // 128):
            tt = (QC * qc) // 128 + ttl
            for j in range(NCT):
                for half in range(2):
                    q = 2 + half
                    P.op("te", (lambda jj=j, t=ttl, hf=half, qq=q: (lambda e: e.matmul(
                        pS[:, 512 * qq:512 * (qq + 1)],
                        ostr[:, jj, 128 * t:128 * (t + 1)],
                        wbuf[:, jj, 512 * hf:512 * (hf + 1)],
                        start=(jj == 0), stop=(jj == NCT - 1))))(),
                        reads=(b_ostr, b_wbuf), writes=(bS[q],), tag=f"wo{qc}_{ttl}_{j}")
            for half in range(2):
                q = 2 + half
                P.op("ve", (lambda t=tt, hf=half, qq=q: (lambda e: e.tensor_tensor(
                    x_tm[:, t, 512 * hf:512 * (hf + 1)],
                    pS[:, 512 * qq:512 * (qq + 1)],
                    x_tm[:, t, 512 * hf:512 * (hf + 1)], ALU.add)))(),
                    reads=(bS[q], b_xtm[tt]), writes=(b_xtm[tt],), tag=f"z{qc}_{ttl}_{half}")
            P.op("sc", (lambda t=tt: (lambda e: e.activation(
                lzsq, x_tm[:, t, :], AF.Square, accum_out=lst[:, t, 0:1])))(),
                reads=(b_xtm[tt],), writes=(b_lzsq, b_lst), tag=f"sq{qc}_{ttl}")
            P.op("ve", (lambda t=tt: (lambda e: e.reduce_sum(
                lst[:, t, 1:2], x_tm[:, t, :], AXX)))(),
                reads=(b_xtm[tt],), writes=(b_lst,), tag=f"sm{qc}_{ttl}")

        if qc == 0:
            C.dump("ostr", ostr, (b_ostr,))
            C.dump("zbc", zbc, (b_zbc,))
    C.dump("zpre", x_tm, tuple(b_xtm))
    C.dump("lst", lst, (b_lst,))
    ln_finish_and_apply(C, lst, lcols, b_lst, b_lcols, b_xtm, x_tm, "l0a")
    C.arena_close()


def build_l0_ffn(C):
    nc, P = C.nc, C.P
    x_tm = C.x_tm
    b_xtm = C.b_xtm
    TC = 256
    NCH = T // TC

    C.arena_open()
    wsum = C.at("wsum", [128, 32], F32)
    scl = C.at("scl", [1, 8], F32)
    sccol = C.at("sccol", [128, 4], F32)
    ccols = C.at("ccols", [128, 32, 3], F32)
    cbcol = C.at("cbcol", [128, 32], F32)
    q1max = C.at("q1max", [128, 16], F32)
    s1cols = C.at("s1cols", [128, 16], F32)
    r1cols = C.at("r1cols", [128, 16], F32)
    r2cols = C.at("r2cols", [128, 16], F32)
    r1row = C.at("r1row", [1, 2056], F32)
    lst = C.at("f_st", [128, NT, 2], F32)
    lcols = C.at("f_cols", [128, NT, 3], F32)
    flsh = C.at("flsh", [128, 256], F32)
    mark = C.arena_mark()
    wtmp = C.at("wtmp", [128, 2, DH], F32)
    wqbf = C.at("wqbf", [128, 2, DH], BF16)
    itile = C.at("itile", [128, D], I32)
    ftile = C.at("ftile", [128, D], F32)
    rtile = C.at("rtile", [128, D], F32)
    gtile = C.at("gtile", [128, D], F32)
    xqt = C.at("xqt", [128, D], BF16)
    lzsq0 = C.at("f_zsq0", [128, D], BF16)

    bufs = {}
    def B(n):
        if n not in bufs:
            bufs[n] = Buf(n)
        return bufs[n]
    pS, pO, pC2, pD, pE = C.pS, C.pO, C.pC, C.pD, C.pE
    bS, bO, bC, bD, bE = C.bS, C.bO, C.bC, C.bD, C.bE
    pAll = [pS[:, 0:512], pS[:, 512:1024], pS[:, 1024:1536], pS[:, 1536:2048],
            pO, pC2, pD, pE]
    bAll = [bS[0], bS[1], bS[2], bS[3], bO, bC, bD, bE]

    def flush(eng="ve"):
        P.op(eng, lambda e: e.memset(flsh, 0.0), writes=(B("flsh"),), tag="flush")
        P.op(eng, lambda e: e.memset(flsh, 0.0), writes=(B("flsh"),), tag="flush2")

    def wprep(which, dst, b_dst, cols, ntile, scl_i, sc_out):
        for i in range(ntile):
            sl = i % 2
            P.op("sy", (lambda ii=i, ss=sl, src=C.win[which], c=cols: (lambda e: e.dma_start(
                out=wtmp[:, ss, 0:c], in_=src[128 * ii:128 * (ii + 1), :])))(),
                reads=(C.b_win[which],), writes=(B("wtmp"),), dma=True, tag=f"wld{which}{i}")
            P.op("ve", (lambda ii=i, ss=sl, c=cols: (lambda e: e.reduce_sum(
                wsum[:, ii:ii + 1], wtmp[:, ss, 0:c], AXX, apply_absolute_value=True)))(),
                reads=(B("wtmp"),), writes=(B("wsum"),), tag=f"wsm{which}{i}")
        flush()
        P.op("te", (lambda nt=ntile: (lambda e: e.matmul(
            pE[0:1, 0:nt], C.s_onescol_f32, wsum[:, 0:nt],
            start=True, stop=True)))(),
            reads=(B("wsum"), C.b_ones), writes=(bE,), tag=f"wtt{which}")
        P.op("ve", (lambda nt=ntile: (lambda e: e.reduce_sum(
            scl[:, scl_i:scl_i + 1], pE[0:1, 0:nt], AXX)))(),
            reads=(bE,), writes=(B("scl"),), tag=f"wt2{which}")
        flush()
        P.op("ve", (lambda c=cols, nt=ntile: (lambda e: e.tensor_scalar(
            scl[:, scl_i + 1:scl_i + 2], scl[:, scl_i:scl_i + 1],
            1.0 / (128.0 * nt * c), 1e-5, ALU.mult, ALU.max)))(),
            reads=(B("scl"),), writes=(B("scl"),), tag=f"wcl{which}")
        flush()
        P.op("ve", lambda e: e.reciprocal(scl[:, scl_i + 2:scl_i + 3], scl[:, scl_i + 1:scl_i + 2]),
             reads=(B("scl"),), writes=(B("scl"),), tag=f"wrc{which}")
        flush()
        P.op("te", lambda e: e.matmul(pE[:, 32:34], C.s_ones_f32,
                                      scl[:, scl_i + 1:scl_i + 3], start=True, stop=True),
             reads=(B("scl"), C.b_ones), writes=(bE,), tag=f"wbc{which}")
        P.op("ve", (lambda so=sc_out: (lambda e: e.tensor_copy(
            sccol[:, so:so + 2], pE[:, 32:34])))(),
            reads=(bE,), writes=(B("sccol"),), tag=f"wbe{which}")
        for i in range(ntile):
            sl = i % 2
            P.op("sy", (lambda ii=i, ss=sl, src=C.win[which], c=cols: (lambda e: e.dma_start(
                out=wtmp[:, ss, 0:c], in_=src[128 * ii:128 * (ii + 1), :])))(),
                reads=(C.b_win[which],), writes=(B("wtmp"),), dma=True, tag=f"wl2{which}{i}")
            eng = "ve"
            P.op(eng, (lambda ss=sl, so=sc_out, c=cols: (lambda e: e.tensor_scalar(
                wtmp[:, ss, 0:c], wtmp[:, ss, 0:c], sccol[:, so + 1:so + 2], None, ALU.mult)))(),
                reads=(B("wtmp"), B("sccol")), writes=(B("wtmp"),), tag=f"wqa{which}{i}")
            P.op(eng, (lambda ss=sl, c=cols: (lambda e: e.tensor_scalar(
                wtmp[:, 1 - ss, 0:c], wtmp[:, ss, 0:c], 0.5, 1.0, ALU.is_gt, ALU.mult)))(),
                reads=(B("wtmp"),), writes=(B("wtmp"),), tag=f"wqb{which}{i}")
            P.op(eng, (lambda ss=sl, c=cols: (lambda e: e.scalar_tensor_tensor(
                wqbf[:, ss, 0:c], wtmp[:, ss, 0:c], -0.5, wtmp[:, 1 - ss, 0:c],
                ALU.is_lt, ALU.subtract)))(),
                reads=(B("wtmp"),), writes=(B("wqbf"),), tag=f"wqc{which}{i}")
            P.op(eng, (lambda ss=sl, c=cols: (lambda e: e.tensor_scalar(
                wqbf[:, ss, 0:c], wqbf[:, ss, 0:c], -1.0, None, ALU.mult)))(),
                reads=(B("wqbf"),), writes=(B("wqbf"),), tag=f"wqd{which}{i}")
            P.op("gp", (lambda ii=i, ss=sl, d=dst, c=cols: (lambda e: e.dma_start(
                out=d[128 * ii:128 * (ii + 1), :], in_=wqbf[:, ss, 0:c])))(),
                reads=(B("wqbf"),), writes=(b_dst,), dma=True, tag=f"wsv{which}{i}")

    wprep("L0_ffn_w1", C.sc_w1q, C.b_scw1q, D, 32, 0, 0)
    wprep("L0_ffn_w2", C.sc_w2q, C.b_scw2q, DH, 8, 3, 2)

    P.op("sy", lambda e: e.dma_start(out=ccols, in_=C.win["L0_ffn_conv_w"].rearrange("(g p) k -> p g k", p=128), allow_slow_non_contiguous=True),
         reads=(C.b_win["L0_ffn_conv_w"],), writes=(B("ccols"),), dma=True, tag="cwl")
    P.op("sy", lambda e: e.dma_start(out=cbcol, in_=C.win["L0_ffn_conv_b"].rearrange("(g p) -> p g", p=128), allow_slow_non_contiguous=True),
         reads=(C.b_win["L0_ffn_conv_b"],), writes=(B("cbcol"),), dma=True, tag="cbl")
    P.op("ve", lambda e: e.tensor_scalar(ccols, ccols, sccol[:, 0:1], None, ALU.mult),
         reads=(B("sccol"), B("ccols")), writes=(B("ccols"),), tag="cfo")

    for tt in range(NT):
        P.op("ve", (lambda t=tt: (lambda e: e.reduce_max(
            q1max[:, t:t + 1], x_tm[:, t, :], AXX, apply_absolute_value=True)))(),
            reads=(b_xtm[tt],), writes=(B("q1max"),), tag=f"q1m{tt}")
    flush()
    P.op("ve", lambda e: e.tensor_scalar(q1max, q1max, 1e-5, None, ALU.max),
         reads=(B("q1max"),), writes=(B("q1max"),), tag="q1c")
    P.op("ve", lambda e: e.reciprocal(s1cols, q1max),
         reads=(B("q1max"),), writes=(B("s1cols"),), tag="q1r")
    flush()
    P.op("ve", lambda e: e.tensor_scalar(s1cols, s1cols, 127.0, None, ALU.mult),
         reads=(B("s1cols"),), writes=(B("s1cols"),), tag="q1s")
    P.op("ve", lambda e: e.tensor_scalar(r1cols, q1max, 1.0 / 127.0, None, ALU.mult),
         reads=(B("q1max"),), writes=(B("r1cols"),), tag="q1rr")
    for tt in range(NT):
        P.op("ve", (lambda t=tt: (lambda e: e.tensor_scalar(
            itile, x_tm[:, t, :], s1cols[:, t:t + 1], 384.5, ALU.mult, ALU.add)))(),
            reads=(b_xtm[tt], B("s1cols")), writes=(B("itile"),), tag=f"xqa{tt}")
        P.op("ve", lambda e: e.tensor_scalar(ftile, itile, 1.0, -384.5, ALU.mult, ALU.add),
             reads=(B("itile"),), writes=(B("ftile"),), tag=f"xqb{tt}")
        P.op("ve", (lambda t=tt: (lambda e: e.scalar_tensor_tensor(
            rtile, x_tm[:, t, :], s1cols[:, t:t + 1], ftile, ALU.mult, ALU.subtract)))(),
            reads=(b_xtm[tt], B("s1cols"), B("ftile")), writes=(B("rtile"),), tag=f"xqc{tt}")
        P.op("ve", lambda e: e.tensor_scalar(gtile, rtile, 0.0, 1.0, ALU.is_lt, ALU.mult),
             reads=(B("rtile"),), writes=(B("gtile"),), tag=f"xqd{tt}")
        P.op("ve", lambda e: e.scalar_tensor_tensor(xqt, ftile, 0.5, gtile, ALU.add, ALU.subtract),
             reads=(B("ftile"), B("gtile")), writes=(B("xqt"),), tag=f"xqe{tt}")
        P.op("gp", (lambda t=tt: (lambda e: e.dma_start(
            out=C.sc_x.rearrange("(a p) d -> p a d", p=128)[:, t, :], in_=xqt)))(),
            reads=(B("xqt"),), writes=(C.b_scx,), dma=True, tag=f"xqw{tt}")
    P.op("ve", lambda e: e.memset(r1row, 0.0), writes=(B("r1row"),), tag="r1z")
    P.op("sy", lambda e: e.dma_start(out=C.sc_mrow[0:16, :].transpose([1, 0]), in_=r1cols,
                                     allow_slow_non_contiguous=True),
         reads=(B("r1cols"),), writes=(C.b_scmd,), dma=True, tag="r1o")
    P.op("sy", lambda e: e.dma_start(out=r1row[:, 1:2049], in_=C.sc_mrow.rearrange("a b -> (a b)")[None, :]),
         reads=(C.b_scmd,), writes=(B("r1row"),), dma=True, tag="r1i")

    C.arena_rewind(mark)
    xq1fm = C.at("xq1fm", [128, 2, NCT, 272], BF16)
    w1t = C.at("w1t", [128, 2, 1024], BF16)
    w2t = C.at("w2t", [128, 2, 1024], BF16)
    hq = C.at("hq", [128, NCT, 258], F32)
    vbuf = C.at("vbuf", [128, 32, TC], F32)
    vi = C.at("vi", [128, NCT, TC], I32)
    vtf = C.at("vtf", [128, NCT, TC], F32)
    xq2c = C.at("xq2c", [128, 32, TC], BF16)
    convt = C.at("convt", [128, 2, TC], F32)
    rmax = C.at("rmax", [128, TC], F32)
    abst = C.at("abst", [128, TC], F32)
    s2row = C.at("s2row", [1, TC], F32)
    s2tok = C.at("s2tok", [128, 2], F32)
    S1b = C.at("S1b", [128, 258], F32)
    S2b = C.at("S2b", [128, TC], F32)
    lzsq = C.at("f_zsq", [128, D], BF16)

    for c in range(NCH):
        w0 = min(max(TC * c - 8, 0), T - 272)
        off = TC * c - 1 - w0
        i0 = 1 if c == 0 else 0
        i1 = 257 if c == NCH - 1 else 258
        sl = c % 2
        for j in range(NCT):
            P.op("sy", (lambda jj=j, ww=w0, ss=sl: (lambda e: e.dma_start_transpose(
                out=xq1fm[:, ss, jj, :], in_=C.sc_x[ww:ww + 272, 128 * jj:128 * (jj + 1)])))(),
                reads=(C.b_scx,), writes=(B("xq1fm"),), dma=True, tag=f"fxq{c}_{j}")
        P.op("te", (lambda cc=c: (lambda e: e.matmul(
            pD[:, 0:258], C.s_ones_f32, r1row[:, TC * cc:TC * cc + 258],
            start=True, stop=True)))(),
            reads=(B("r1row"), C.b_ones), writes=(bD,), tag=f"s1b{c}")
        P.op("ve", lambda e: e.tensor_copy(S1b, pD[:, 0:258]),
             reads=(bD,), writes=(B("S1b"),), tag=f"s1e{c}")
        for qtr in range(4):
            for k in range(NCT):
                ws = k % 2
                P.op("sy", (lambda q=qtr, kk=k, ss=ws: (lambda e: e.dma_start_transpose(
                    out=w1t[:, ss, :], in_=C.sc_w1q[1024 * q:1024 * (q + 1), 128 * kk:128 * (kk + 1)])))(),
                    reads=(C.b_scw1q,), writes=(B("w1t"),), dma=True, tag=f"w1l{c}_{qtr}_{k}")
                for m in range(NCT):
                    P.op("te", (lambda mm=m, kk=k, ss=ws, s2=sl, o=off, a=i0, b2=i1:
                                (lambda e: e.matmul(
                        pAll[mm][:, a:b2], w1t[:, ss, 128 * mm:128 * (mm + 1)],
                        xq1fm[:, s2, kk, o + a:o + b2],
                        start=(kk == 0), stop=(kk == NCT - 1))))(),
                        reads=(B("w1t"), B("xq1fm")), writes=(bAll[m],), tag=f"hmm{c}_{qtr}_{k}_{m}")
            for m in range(NCT):
                g = 8 * qtr + m
                P.op("ve", (lambda mm=m, a=i0, b2=i1: (lambda e: e.tensor_tensor(
                    hq[:, mm, a:b2], pAll[mm][:, a:b2], S1b[:, a:b2], ALU.mult)))(),
                    reads=(bAll[m], B("S1b")), writes=(B("hq"),), tag=f"hev{c}_{g}")
                if c == 0:
                    P.op("ve", (lambda mm=m: (lambda e: e.memset(hq[:, mm, 0:1], 0.0)))(),
                         writes=(B("hq"),), tag=f"hz0{g}")
                if c == NCH - 1:
                    P.op("ve", (lambda mm=m: (lambda e: e.memset(hq[:, mm, 257:258], 0.0)))(),
                         writes=(B("hq"),), tag=f"hz1{g}")
                if C.debug and c == 0 and g == 0:
                    C.dump("hq_first", hq, (B("hq"),))
                    C.dump("ccols_d", ccols, (B("ccols"),))
                    C.dump("sccol_d", sccol, (B("sccol"),))
                eng = "ve"
                ct = convt[:, g % 2, :]
                P.op(eng, (lambda mm=m, gg=g, cc=ct: (lambda e: e.tensor_scalar(
                    cc, hq[:, mm, 1:257], ccols[:, gg, 1:2], cbcol[:, gg:gg + 1],
                    ALU.mult, ALU.add)))(),
                    reads=(B("hq"), B("ccols"), B("cbcol")), writes=(B("convt"),), tag=f"cv1{c}_{g}")
                P.op(eng, (lambda mm=m, gg=g, cc=ct: (lambda e: e.scalar_tensor_tensor(
                    cc, hq[:, mm, 0:256], ccols[:, gg, 0:1], cc, ALU.mult, ALU.add)))(),
                    reads=(B("hq"), B("ccols"), B("convt")), writes=(B("convt"),), tag=f"cv2{c}_{g}")
                P.op(eng, (lambda mm=m, gg=g, cc=ct: (lambda e: e.scalar_tensor_tensor(
                    cc, hq[:, mm, 2:258], ccols[:, gg, 2:3], cc, ALU.mult, ALU.add)))(),
                    reads=(B("hq"), B("ccols"), B("convt")), writes=(B("convt"),), tag=f"cv3{c}_{g}")
                if C.debug and c == 0 and g == 0:
                    C.dump("convt_d", convt, (B("convt"),))
                P.op("sc", (lambda gg=g, cc=ct: (lambda e: e.activation(
                    vbuf[:, gg, :], cc, AF.Silu)))(),
                    reads=(B("convt"),), writes=(B("vbuf"),), tag=f"sil{c}_{g}")
                P.op("sc", (lambda gg=g: (lambda e: e.activation(
                    abst, vbuf[:, gg, :], AF.Abs)))(),
                    reads=(B("vbuf"),), writes=(B("abst"),), tag=f"ab{c}_{g}")
                if g == 0:
                    P.op("ve", lambda e: e.memset(rmax, 0.0),
                         writes=(B("rmax"),), tag=f"rmz{c}")
                P.op("ve", (lambda gg=g: (lambda e: e.tensor_tensor(
                    rmax, rmax, abst, ALU.max)))(),
                    reads=(B("abst"), B("rmax")), writes=(B("rmax"),), tag=f"rm{c}_{g}")
        if C.debug and c == 0:
            C.dump("hq0", hq, (B("hq"),))
            C.dump("vbuf0", vbuf, (B("vbuf"),))
            C.dump("rmax0", rmax, (B("rmax"),))
        for half in range(2):
            P.op("te", (lambda hh=half: (lambda e: e.transpose(
                pD[:, 128 * hh:128 * (hh + 1)], rmax[:, 128 * hh:128 * (hh + 1)],
                C.s_idf)))(),
                reads=(B("rmax"), C.b_ones), writes=(bD,), tag=f"s2t{c}_{half}")
            P.op("ve", (lambda hh=half: (lambda e: e.reduce_max(
                s2tok[:, hh:hh + 1], pD[:, 128 * hh:128 * (hh + 1)], AXX)))(),
                reads=(bD,), writes=(B("s2tok"),), tag=f"s2r{c}_{half}")
        flush()
        P.op("ve", lambda e: e.tensor_scalar(s2tok, s2tok, 1e-5, None, ALU.max),
             reads=(B("s2tok"),), writes=(B("s2tok"),), tag=f"s2c{c}")
        P.op("ve", (lambda cc=c: (lambda e: e.tensor_scalar(
            r2cols[:, 2 * cc:2 * cc + 2], s2tok, 1.0 / 127.0, None, ALU.mult)))(),
            reads=(B("s2tok"),), writes=(B("r2cols"),), tag=f"r2a{c}")
        P.op("ve", (lambda cc=c: (lambda e: e.tensor_scalar(
            r2cols[:, 2 * cc:2 * cc + 2], r2cols[:, 2 * cc:2 * cc + 2], sccol[:, 2:3], None,
            ALU.mult)))(),
            reads=(B("r2cols"), B("sccol")), writes=(B("r2cols"),), tag=f"r2b{c}")
        P.op("ve", lambda e: e.reciprocal(s2tok, s2tok),
             reads=(B("s2tok"),), writes=(B("s2tok"),), tag=f"s2i{c}")
        flush()
        P.op("ve", lambda e: e.tensor_scalar(s2tok, s2tok, 127.0, None, ALU.mult),
             reads=(B("s2tok"),), writes=(B("s2tok"),), tag=f"s2s{c}")
        P.op("sy", lambda e: e.dma_start(out=C.sc_mrow[0:2, :].transpose([1, 0]), in_=s2tok,
                                         allow_slow_non_contiguous=True),
             reads=(B("s2tok"),), writes=(C.b_scmd,), dma=True, tag=f"s2o{c}")
        P.op("sy", lambda e: e.dma_start(out=s2row, in_=C.sc_mrow.rearrange("a b -> (a b)")[None, 0:TC]),
             reads=(C.b_scmd,), writes=(B("s2row"),), dma=True, tag=f"s2n{c}")
        P.op("te", lambda e: e.matmul(pD[:, 0:TC], C.s_ones_f32, s2row, start=True, stop=True),
             reads=(B("s2row"), C.b_ones), writes=(bD,), tag=f"s2b{c}")
        P.op("ve", lambda e: e.tensor_copy(S2b, pD[:, 0:TC]),
             reads=(bD,), writes=(B("S2b"),), tag=f"s2e{c}")
        if C.debug and c == 0:
            C.dump("s2tok0", s2tok, (B("s2tok"),))
            C.dump("S2b0", S2b, (B("S2b"),))
        for grp in range(4):
            eng = "ve"
            vb = vbuf[:, 8 * grp:8 * (grp + 1), :]
            vif = vi[:, :, :]
            vtff = vtf[:, :, :]
            xqf = xq2c[:, 8 * grp:8 * (grp + 1), :]
            s2bb = S2b[:, None, :].broadcast_to((128, NCT, TC))
            P.op(eng, (lambda vb=vb, s2bb=s2bb: (lambda e: e.tensor_tensor(vb, vb, s2bb, ALU.mult)))(),
                 reads=(B("vbuf"), B("S2b")), writes=(B("vbuf"),), tag=f"rn1{c}_{grp}")
            P.op(eng, (lambda vb=vb, vif=vif: (lambda e: e.tensor_scalar(vif, vb, 1.0, 384.5, ALU.mult, ALU.add)))(),
                 reads=(B("vbuf"),), writes=(B("vi"),), tag=f"rn2{c}_{grp}")
            P.op(eng, (lambda vif=vif, vtff=vtff: (lambda e: e.tensor_scalar(vtff, vif, 1.0, -384.5, ALU.mult, ALU.add)))(),
                 reads=(B("vi"),), writes=(B("vtf"),), tag=f"rn3{c}_{grp}")
            P.op(eng, (lambda vb=vb, vtff=vtff: (lambda e: e.tensor_tensor(vb, vb, vtff, ALU.subtract)))(),
                 reads=(B("vbuf"), B("vtf")), writes=(B("vbuf"),), tag=f"rn4{c}_{grp}")
            P.op(eng, (lambda vb=vb: (lambda e: e.tensor_scalar(vb, vb, 0.0, 1.0, ALU.is_lt, ALU.mult)))(),
                 reads=(B("vbuf"),), writes=(B("vbuf"),), tag=f"rn5{c}_{grp}")
            P.op(eng, (lambda vb=vb, vtff=vtff, xqf=xqf: (lambda e: e.scalar_tensor_tensor(
                xqf, vtff, 0.5, vb, ALU.add, ALU.subtract)))(),
                reads=(B("vtf"), B("vbuf")), writes=(B("xq2c"),), tag=f"rn6{c}_{grp}")
        if C.debug and c == 0:
            C.dump("xq2c0", xq2c, (B("xq2c"),))
        for kt in range(32):
            ws = kt % 2
            P.op("sy", (lambda k=kt, ss=ws: (lambda e: e.dma_start_transpose(
                out=w2t[:, ss, :], in_=C.sc_w2q[:, 128 * k:128 * (k + 1)])))(),
                reads=(C.b_scw2q,), writes=(B("w2t"),), dma=True, tag=f"w2l{c}_{kt}")
            for ttl in range(2):
                for half in range(2):
                    P.op("te", (lambda k=kt, ss=ws, t=ttl, hf=half: (lambda e: e.matmul(
                        pS[:, 1024 * t + 512 * hf:1024 * t + 512 * (hf + 1)],
                        xq2c[:, k, 128 * t:128 * (t + 1)],
                        w2t[:, ss, 512 * hf:512 * (hf + 1)],
                        start=(k == 0), stop=(k == 31))))(),
                        reads=(B("w2t"), B("xq2c")), writes=(bS[2 * ttl + half],), tag=f"h2m{c}_{kt}_{ttl}_{half}")
        for ttl in range(2):
            tt = 2 * c + ttl
            for half in range(2):
                P.op("ve", (lambda t=tt, tl=ttl, hf=half: (lambda e: e.scalar_tensor_tensor(
                    x_tm[:, t, 512 * hf:512 * (hf + 1)],
                    pS[:, 1024 * tl + 512 * hf:1024 * tl + 512 * (hf + 1)],
                    r2cols[:, t:t + 1], x_tm[:, t, 512 * hf:512 * (hf + 1)],
                    ALU.mult, ALU.add)))(),
                    reads=(bS[2 * ttl + half], B("r2cols"), b_xtm[tt]), writes=(b_xtm[tt],), tag=f"z2{tt}_{half}")
            P.op("sc", (lambda t=tt: (lambda e: e.activation(
                lzsq, x_tm[:, t, :], AF.Square, accum_out=lst[:, t, 0:1])))(),
                reads=(b_xtm[tt],), writes=(B("f_zsq"), B("lst")), tag=f"fsq{tt}")
            P.op("ve", (lambda t=tt: (lambda e: e.reduce_sum(
                lst[:, t, 1:2], x_tm[:, t, :], AXX)))(),
                reads=(b_xtm[tt],), writes=(B("lst"),), tag=f"fsm{tt}")

    C.dump("fsccol", sccol, (B("sccol"),))
    C.dump("fscl", scl, (B("scl"),))
    C.dump("fq1max", q1max, (B("q1max"),))
    C.dump("fr1", r1cols, (B("r1cols"),))
    C.dump("fr2", r2cols, (B("r2cols"),))
    C.dump("fzpre", x_tm, tuple(b_xtm))
    ln_finish_and_apply(C, lst, lcols, B("lst"), B("lcols"), b_xtm, x_tm, "l0f")
    C.arena_close()


class _Stop(Exception):
    pass


def build_l1_attn(C):
    _cut = int(os.environ.get("BK_CUT", "999999"))
    _cutn = [0]

    def CP():
        _cutn[0] += 1
        if _cutn[0] > _cut:
            raise _Stop()
    try:
        _build_l1_attn(C, CP)
    except _Stop:
        print(f"l1a truncated at checkpoint {_cut}")
    C.arena_close()


def _build_l1_attn(C, CP):
    nc, P = C.nc, C.P
    x_tm, xfm = C.x_tm, C.xfm
    b_xtm, b_xfm = C.b_xtm, C.b_xfm
    MG = 512
    NMG = T // MG

    C.arena_open()
    q_fm = C.at("q_fm", [128, NCT, MG], BF16)
    k_fm = C.at("k_fm", [128, NCT, MG], BF16)
    v_tm = C.at("v_tm", [128, 4, D], BF16)
    ka_tm = C.at("ka_tm", [128, 4, D], BF16)
    gate_fm = C.at("gate_fm", [128, NCT, MG], BF16)
    dx_fm = C.at("dx_fm", [128, NCT, MG], BF16)
    xf = C.at("xf", [128, 2, NCT, MG], BF16)
    p_tm = C.at("p_tm", [128, 1, D], F32)
    mbuf = C.at("mbuf", [128, MG], F32)
    mxs = C.at("mxs", [32, 5, MG], BF16)
    thw = C.at("thw", [64, MG], BF16)
    S_all = C.at("S_all", [128, NCT, 64], F32)
    S_bf = C.at("S_bf", [128, NCT, 64], BF16)
    P_sb = C.at("P_sb", [128, H, 64], BF16)
    Ep = C.at("Ep", [128, NCT, 128], BF16)
    Em = C.at("Em", [128, NCT, 128], BF16)
    ecols = C.at("ecols", [128, NCT, 16], F32)
    wstr = C.at("wstr", [128, 2, D], BF16)
    vcols = C.at("vcols", [128, NCT, 8], F32)
    vrows = C.at("vrows", [8, D], F32)
    TDb = C.at("TDb", [128, D], F32)
    mw1 = C.at("mw1", [128, NCT, 160], BF16)
    dw1 = C.at("dw1", [128, NCT, 64], BF16)
    dw2 = C.at("dw2", [64, D], BF16)
    mw2s = C.at("mw2s", [32, D], BF16)
    ubd = C.at("ubd", [128, 128], F32)
    vsel = C.at("vsel", [128, 4], F32)
    trim = C.at("trim", [128, 64], BF16)
    mln8 = C.at("mln8", [128, 1], F32)
    tdcols = C.at("tdcols", [128, 8], F32)
    lst = C.at("a_st", [128, NT, 2], F32)
    lcols = C.at("a_cols", [128, NT, 3], F32)
    lzsq = mbuf.bitcast(BF16)

    bufs = {}
    def B(n):
        if n not in bufs:
            bufs[n] = Buf(n)
        return bufs[n]
    pS, pO, pC2, pD, pE = C.pS, C.pO, C.pC, C.pD, C.pE
    bS, bO, bC, bD, bE = C.bS, C.bO, C.bC, C.bD, C.bE
    pAll = [pS[:, 0:512], pS[:, 512:1024], pS[:, 1024:1536], pS[:, 1536:2048],
            pO, pC2, pD, pE]
    bAll = [bS[0], bS[1], bS[2], bS[3], bO, bC, bD, bE]

    _u = np.zeros((128, 128), dtype=np.float32)
    for blk in range(2):
        o = 64 * blk
        for t in range(64):
            for u in range(64):
                if t < u <= 31:
                    _u[o + u, o + t] = -1.0
                elif 31 < u <= t:
                    _u[o + u, o + t] = 1.0
    ubd_c = nc.inline_tensor(_u, name="ubd_c").ap()
    _v = np.zeros((128, 4), dtype=np.float32)
    for blk in range(2):
        o = 64 * blk
        _v[o:o + 32, 2 * blk] = 1.0
        _v[o + 32:o + 64, 2 * blk + 1] = 1.0
    vsel_c = nc.inline_tensor(_v, name="vsel_c").ap()
    _t = np.zeros((128, 64), dtype=ml_dtypes.bfloat16)
    for p in range(128):
        for t in range(p % 64, 64):
            _t[p, t] = 1.0
    trim_c = nc.inline_tensor(_t, name="trim_c").ap()

    P.op("sy", lambda e: e.dma_start(out=ubd, in_=ubd_c), writes=(B("ubd"),), dma=True, tag="lubd")
    P.op("sy", lambda e: e.dma_start(out=vsel, in_=vsel_c), writes=(B("vsel"),), dma=True, tag="lvsel")
    P.op("sy", lambda e: e.dma_start(out=trim, in_=trim_c), writes=(B("trim"),), dma=True, tag="ltrim")
    P.op("ve", lambda e: e.memset(mln8, -2.0794415416798357), writes=(B("mln8"),), tag="lmln8")

    vec_names = ["L1_attn_time_maa_x", "L1_attn_time_maa_r", "L1_attn_time_maa_k",
                 "L1_attn_time_maa_v", "L1_attn_time_maa_w", "L1_attn_time_maa_g",
                 "L1_attn_bg"]
    for i, nm in enumerate(vec_names):
        P.op("sy", (lambda n=nm, ii=i: (lambda e: e.dma_start(
            out=vrows[ii:ii + 1, :], in_=C.win[n][None, :])))(),
            reads=(C.b_win[nm],), writes=(B("vrows"),), dma=True, tag=f"lvr{i}")
    for j in range(NCT):
        P.op("te", (lambda jj=j: (lambda e: e.transpose(
            pD[:, 0:8], vrows[0:8, 128 * jj:128 * (jj + 1)], C.s_idf[0:8, 0:8])))(),
            reads=(B("vrows"), C.b_ones), writes=(bD,), tag=f"vct{j}")
        P.op("ve", (lambda jj=j: (lambda e: e.tensor_copy(vcols[:, jj, :], pD[:, 0:8])))(),
             reads=(bD,), writes=(B("vcols"),), tag=f"vce{j}")
    P.op("sy", lambda e: e.dma_start(out=vrows[0:1, :], in_=C.win["L1_attn_time_decay"][None, :]),
         reads=(C.b_win["L1_attn_time_decay"],), writes=(B("vrows"),), dma=True, tag="ltd")
    for hh in range(2):
        P.op("te", (lambda h2=hh: (lambda e: e.matmul(
            pD[:, 0:512], C.s_ones_f32, vrows[0:1, 512 * h2:512 * (h2 + 1)],
            start=True, stop=True)))(),
            reads=(B("vrows"), C.b_ones), writes=(bD,), tag=f"tdb{hh}")
        P.op("ve", (lambda h2=hh: (lambda e: e.tensor_copy(
            TDb[:, 512 * h2:512 * (h2 + 1)], pD[:, 0:512])))(),
            reads=(bD,), writes=(B("TDb"),), tag=f"tde{hh}")
    for j in range(NCT):
        P.op("te", (lambda jj=j: (lambda e: e.transpose(
            pD[:, jj:jj + 1], vrows[0:1, 128 * jj:128 * (jj + 1)], C.s_idf[0:1, 0:1])))(),
            reads=(B("vrows"), C.b_ones), writes=(bD,), tag=f"tdT{j}")
    P.op("ve", lambda e: e.tensor_copy(tdcols, pD[:, 0:8]),
         reads=(bD,), writes=(B("tdcols"),), tag="tdcE")

    P.op("gp", lambda e: e.dma_start(
        out=mw1, in_=C.win["L1_attn_time_maa_w1"].rearrange("(kt p) m -> p kt m", p=128)),
        reads=(C.b_win["L1_attn_time_maa_w1"],), writes=(B("mw1"),), dma=True, tag="lmw1")
    P.op("gp", lambda e: e.dma_start(
        out=dw1, in_=C.win["L1_attn_time_decay_w1"].rearrange("(kt p) m -> p kt m", p=128)),
        reads=(C.b_win["L1_attn_time_decay_w1"],), writes=(B("dw1"),), dma=True, tag="ldw1")
    P.op("gp", lambda e: e.dma_start(out=dw2, in_=C.win["L1_attn_time_decay_w2"]),
         reads=(C.b_win["L1_attn_time_decay_w2"],), writes=(B("dw2"),), dma=True, tag="ldw2")

    for nm in ("Wq", "Wk", "Wv", "Wg", "Wo"):
        P.op("gp", (lambda n=nm: (lambda e: e.dma_start(
            out=C.sc_l1w[n], in_=C.win["L1_attn_" + n])))(),
            reads=(C.b_win["L1_attn_" + nm],), writes=(C.b_scl1w[nm],), dma=True, tag=f"lc{nm}")

    P.op("ve", lambda e: e.memset(S_all, 0.0), writes=(B("S_all"),), tag="s0z")
    P.op("ve", lambda e: e.memset(S_bf, 0.0), writes=(B("S_bf"),), tag="s0zb")

    def stream_w(nm, kt, slot, tag):
        P.op("sy", (lambda n=nm, k=kt, ss=slot: (lambda e: e.dma_start(
            out=wstr[:, ss, :], in_=C.sc_l1w[n][128 * k:128 * (k + 1), :])))(),
            reads=(C.b_scl1w[nm],), writes=(B("wstr"),), dma=True, tag=tag)

    CP()
    for mg in range(NMG):
        c0 = MG * mg
        for j in range(NCT):
            if mg == 0:
                P.op("ve", (lambda jj=j: (lambda e: e.tensor_tensor(
                    dx_fm[:, jj, 1:MG], xfm[:, jj, 0:MG - 1], xfm[:, jj, 1:MG],
                    ALU.subtract)))(),
                    reads=(b_xfm[j],), writes=(B("dx_fm"),), tag=f"dx{mg}_{j}")
                P.op("ve", (lambda jj=j: (lambda e: e.tensor_scalar(
                    dx_fm[:, jj, 0:1], xfm[:, jj, 0:1], -1.0, None, ALU.mult)))(),
                    reads=(b_xfm[j],), writes=(B("dx_fm"),), tag=f"dx0{mg}_{j}")
            else:
                P.op("ve", (lambda jj=j, cc=c0: (lambda e: e.tensor_tensor(
                    dx_fm[:, jj, :], xfm[:, jj, cc - 1:cc + MG - 1], xfm[:, jj, cc:cc + MG],
                    ALU.subtract)))(),
                    reads=(b_xfm[j],), writes=(B("dx_fm"),), tag=f"dx{mg}_{j}")
        for j in range(NCT):
            P.op("ve", (lambda jj=j, cc=c0: (lambda e: e.scalar_tensor_tensor(
                xf[:, 1, jj, :], dx_fm[:, jj, :], vcols[:, jj, 0:1], xfm[:, jj, cc:cc + MG],
                ALU.mult, ALU.add)))(),
                reads=(B("dx_fm"), B("vcols"), b_xfm[j]), writes=(B("xf"),), tag=f"xxx{mg}_{j}")
        CP()
        if C.debug and mg == 0:
            C.dump("a1xf_xxx", xf[:, 1], (B("xf"),))
        pmx = [pS[:, 0:512], pS[:, 512:1024], pS[:, 1024:1536], pS[:, 1536:2048], pO]
        bmx = [bS[0], bS[1], bS[2], bS[3], bO]
        for f in range(5):
            for kt in range(NCT):
                P.op("te", (lambda ff=f, k=kt: (lambda e: e.matmul(
                    pmx[ff][0:32, 0:MG], mw1[:, k, 32 * ff:32 * (ff + 1)],
                    xf[:, 1, k, :], start=(k == 0), stop=(k == NCT - 1))))(),
                    reads=(B("mw1"), B("xf")), writes=(bmx[f],), tag=f"mxm{mg}_{f}_{kt}")
            P.op("sc", (lambda ff=f: (lambda e: e.activation(
                mxs[:, ff, :], pmx[ff][0:32, 0:MG], AF.Tanh)))(),
                reads=(bmx[f],), writes=(B("mxs"),), tag=f"mxt{mg}_{f}")

        if C.debug and mg == 0:
            C.dump("a1mxs", mxs, (B("mxs"),))
        fmap = [("r", 1, "Wq", "q"), ("k", 2, "Wk", "k"), ("v", 3, "Wv", "v"),
                ("w", 4, None, "w"), ("g", 5, "Wg", "g")]
        for fi, (fn, vci, wnm, kind) in enumerate(fmap):
            CP()
            sl = fi % 2
            P.op("gp", (lambda ff=fi: (lambda e: e.dma_start(
                out=mw2s, in_=C.win["L1_attn_time_maa_w2"][ff])))(),
                reads=(C.b_win["L1_attn_time_maa_w2"],), writes=(B("mw2s"),), dma=True, tag=f"lm2{mg}_{fi}")
            for j in range(NCT):
                pmf = pD if j % 2 == 0 else pE
                bmf = bD if j % 2 == 0 else bE
                P.op("te", (lambda ff=fi, jj=j, pp=pmf: (lambda e: e.matmul(
                    pp[:, 0:MG], mw2s[:, 128 * jj:128 * (jj + 1)], mxs[:, ff, :],
                    start=True, stop=True)))(),
                    reads=(B("mw2s"), B("mxs")), writes=(bmf,), tag=f"mpm{mg}_{fi}_{j}")
                P.op("sc", (lambda jj=j, vc=vci, pp=pmf: (lambda e: e.activation(
                    mbuf, pp[:, 0:MG], AF.Identity, bias=vcols[:, jj, vc:vc + 1])))(),
                    reads=(bmf, B("vcols")), writes=(B("mbuf"),), tag=f"mpe{mg}_{fi}_{j}")
                P.op("ve", (lambda jj=j: (lambda e: e.tensor_tensor(
                    mbuf, dx_fm[:, jj, :], mbuf, ALU.mult)))(),
                    reads=(B("dx_fm"), B("mbuf")), writes=(B("mbuf"),), tag=f"mpx{mg}_{fi}_{j}")
                P.op("ve", (lambda jj=j, ss=sl, cc=c0: (lambda e: e.tensor_tensor(
                    xf[:, ss, jj, :], xfm[:, jj, cc:cc + MG], mbuf, ALU.add)))(),
                    reads=(b_xfm[j], B("mbuf")), writes=(B("xf"),), tag=f"mpa{mg}_{fi}_{j}")
            if kind in ("q", "k", "g"):
                dst = {"q": q_fm, "k": k_fm, "g": gate_fm}[kind]
                bdst = B({"q": "q_fm", "k": "k_fm", "g": "gate_fm"}[kind])
                for kt in range(NCT):
                    stream_w(wnm, kt, kt % 2, f"sw{mg}_{fi}_{kt}")
                    for j in range(NCT):
                        pj = pAll[j][:, 0:MG]
                        bj = bAll[j]
                        P.op("te", (lambda jj=j, k=kt, ss=kt % 2, s2=sl, pp=pj: (lambda e: e.matmul(
                            pp, wstr[:, ss, 128 * jj:128 * (jj + 1)], xf[:, s2, k, :],
                            start=(k == 0), stop=(k == NCT - 1))))(),
                            reads=(B("wstr"), B("xf")), writes=(bj,), tag=f"pj{mg}_{fi}_{kt}_{j}")
                for j in range(NCT):
                    pj = pAll[j][:, 0:MG]
                    bj = bAll[j]
                    if kind == "g":
                        P.op("sc", (lambda jj=j, pp=pj: (lambda e: e.activation(
                            gate_fm[:, jj, :], pp, AF.Silu, bias=vcols[:, jj, 6:7])))(),
                            reads=(bj, B("vcols")), writes=(bdst,), tag=f"pje{mg}_{fi}_{j}")
                    else:
                        P.op("ve", (lambda jj=j, pp=pj, d=dst: (lambda e: e.tensor_copy(
                            d[:, jj, :], pp)))(),
                            reads=(bj,), writes=(bdst,), tag=f"pje{mg}_{fi}_{j}")
                if C.debug and mg == 0:
                    C.dump("a1" + kind + "fm", dst, (bdst,))
                    C.dump("a1xf_" + fn, xf[:, sl], (B("xf"),))
            elif kind == "v":
                for kt in range(NCT):
                    stream_w("Wv", kt, kt % 2, f"swv{mg}_{kt}")
                    for ttl in range(4):
                        for hf in range(2):
                            pj = pAll[2 * ttl + hf][:, 0:512]
                            bj = bAll[2 * ttl + hf]
                            P.op("te", (lambda t=ttl, k=kt, ss=kt % 2, s2=sl, h2=hf, pp=pj: (lambda e: e.matmul(
                                pp, xf[:, s2, k, 128 * t:128 * (t + 1)],
                                wstr[:, ss, 512 * h2:512 * (h2 + 1)],
                                start=(k == 0), stop=(k == NCT - 1))))(),
                                reads=(B("wstr"), B("xf")), writes=(bj,), tag=f"pv{mg}_{kt}_{ttl}_{hf}")
                for ttl in range(4):
                    for hf in range(2):
                        pj = pAll[2 * ttl + hf][:, 0:512]
                        bj = bAll[2 * ttl + hf]
                        P.op("ve", (lambda t=ttl, h2=hf, pp=pj: (lambda e: e.tensor_copy(
                            v_tm[:, t, 512 * h2:512 * (h2 + 1)], pp)))(),
                            reads=(bj,), writes=(B("v_tm"),), tag=f"pve{mg}_{ttl}_{hf}")
                if C.debug and mg == 0:
                    C.dump("a1vtm", v_tm, (B("v_tm"),))
                    C.dump("a1xf_v", xf[:, sl], (B("xf"),))
            else:
                for kt in range(NCT):
                    P.op("te", (lambda k=kt, s2=sl: (lambda e: e.matmul(
                        pO[0:64, 0:MG], dw1[:, k, :], xf[:, s2, k, :],
                        start=(k == 0), stop=(k == NCT - 1))))(),
                        reads=(B("dw1"), B("xf")), writes=(bO,), tag=f"pw{mg}_{kt}")
                P.op("sc", lambda e: e.activation(thw, pO[0:64, 0:MG], AF.Tanh),
                     reads=(bO,), writes=(B("thw"),), tag=f"pwe{mg}")
                for ttl in range(4):
                    sp = 0
                    for hf in range(2):
                        P.op("te", (lambda t=ttl, h2=hf: (lambda e: e.matmul(
                            pS[:, 512 * h2:512 * (h2 + 1)],
                            thw[:, 128 * t:128 * (t + 1)],
                            dw2[:, 512 * h2:512 * (h2 + 1)],
                            start=True, stop=True)))(),
                            reads=(B("thw"), B("dw2")), writes=(bS[hf],), tag=f"dtm{mg}_{ttl}_{hf}")
                    P.op("ve", (lambda ss=sp: (lambda e: e.tensor_tensor(
                        p_tm[:, ss, :], pS[:, 0:1024], TDb, ALU.add)))(),
                        reads=(bS[0], bS[1], B("TDb")), writes=(B("p_tm"),), tag=f"dta{mg}_{ttl}")
                    P.op("sc", (lambda ss=sp: (lambda e: e.activation(
                        p_tm[:, ss, :], p_tm[:, ss, :], AF.Exp)))(),
                        reads=(B("p_tm"),), writes=(B("p_tm"),), tag=f"dte{mg}_{ttl}")
                    for j in range(NCT):
                        P.op("te", (lambda jj=j, ss=sp: (lambda e: e.matmul(
                            pC2[:, 4 * jj:4 * (jj + 1)], p_tm[:, ss, 128 * jj:128 * (jj + 1)],
                            vsel, start=True, stop=True)))(),
                            reads=(B("p_tm"), B("vsel")), writes=(bC,), tag=f"pt{mg}_{ttl}_{j}")
                    P.op("sc", (lambda t=ttl: (lambda e: e.activation(
                        ecols[:, :, 4 * t:4 * (t + 1)],
                        pC2[:, 0:32].rearrange("p (j s) -> p j s", s=4),
                        AF.Exp, scale=-1.0)))(),
                        reads=(bC,), writes=(B("ecols"),), tag=f"pte{mg}_{ttl}")
                    for jp in range(4):
                        pb2 = pD if jp % 2 == 0 else pE
                        bb2 = bD if jp % 2 == 0 else bE
                        for jo in range(2):
                            j = 2 * jp + jo
                            P.op("te", (lambda jj=j, ss=sp, o=jo, pp=pb2: (lambda e: e.matmul(
                                pp[:, 128 * o:128 * (o + 1)],
                                p_tm[:, ss, 128 * jj:128 * (jj + 1)],
                                ubd, start=True, stop=True)))(),
                                reads=(B("p_tm"), B("ubd")), writes=(bb2,), tag=f"bm{mg}_{ttl}_{j}")
                        P.op("sc", (lambda jp2=jp, pp=pb2: (lambda e: e.activation(
                            Ep[:, 2 * jp2:2 * jp2 + 2, :].rearrange("p a b -> p (a b)"),
                            pp[:, 0:256], AF.Exp)))(),
                            reads=(bb2,), writes=(B("Ep"),), tag=f"bep{mg}_{ttl}_{jp}")
                        P.op("sc", (lambda jp2=jp, pp=pb2: (lambda e: e.activation(
                            Em[:, 2 * jp2:2 * jp2 + 2, :].rearrange("p a b -> p (a b)"),
                            pp[:, 0:256], AF.Exp, scale=-1.0,
                            bias=mln8[:, 0:1])))(),
                            reads=(bb2, B("mln8")), writes=(B("Em"),), tag=f"bem{mg}_{ttl}_{jp}")
                    for j in range(NCT):
                        P.op("te", (lambda jj=j, t=ttl: (lambda e: e.matmul(
                            pS[:, 1024 + 128 * jj:1024 + 128 * (jj + 1)],
                            dw2[:, 128 * jj:128 * (jj + 1)],
                            thw[:, 128 * t:128 * (t + 1)],
                            start=True, stop=True)))(),
                            reads=(B("dw2"), B("thw")), writes=(bS[2], bS[3]), tag=f"am{mg}_{ttl}_{j}")
                    P.op("ve", lambda e: e.tensor_tensor(
                        pS[:, 1024:2048].rearrange("p (a b) -> p a b", b=128),
                        pS[:, 1024:2048].rearrange("p (a b) -> p a b", b=128),
                        tdcols[:, :, None].broadcast_to((128, NCT, 128)),
                        ALU.add),
                        reads=(bS[2], bS[3], B("tdcols")), writes=(bS[2], bS[3]), tag=f"atd{mg}_{ttl}")
                    P.op("sc", lambda e: e.activation(
                        pS[:, 1024:2048], pS[:, 1024:2048], AF.Exp),
                        reads=(bS[2], bS[3]), writes=(bS[2], bS[3]), tag=f"ae1{mg}_{ttl}")
                    P.op("sc", lambda e: e.activation(
                        pS[:, 1024:2048], pS[:, 1024:2048], AF.Exp, scale=-1.0),
                        reads=(bS[2], bS[3]), writes=(bS[2], bS[3]), tag=f"ae2{mg}_{ttl}")
                    P.op("ve", lambda e: e.tensor_scalar(
                        pS[:, 1024:2048], pS[:, 1024:2048], -1.0, 1.0, ALU.mult, ALU.add),
                        reads=(bS[2], bS[3]), writes=(bS[2], bS[3]), tag=f"ae3{mg}_{ttl}")
                    P.op("ve", (lambda t=ttl: (lambda e: e.tensor_tensor(
                        q_fm[:, :, 128 * t:128 * (t + 1)],
                        q_fm[:, :, 128 * t:128 * (t + 1)], Em, ALU.mult)))(),
                        reads=(B("q_fm"), B("Em")), writes=(B("q_fm"),), tag=f"qt{mg}_{ttl}")
                    P.op("ve", (lambda t=ttl: (lambda e: e.tensor_tensor(
                        k_fm[:, :, 128 * t:128 * (t + 1)],
                        k_fm[:, :, 128 * t:128 * (t + 1)], Ep, ALU.mult)))(),
                        reads=(B("k_fm"), B("Ep")), writes=(B("k_fm"),), tag=f"ka{mg}_{ttl}")
                    P.op("ve", (lambda t=ttl: (lambda e: e.tensor_tensor(
                        k_fm[:, :, 128 * t:128 * (t + 1)],
                        k_fm[:, :, 128 * t:128 * (t + 1)],
                        pS[:, 1024:2048].rearrange("p (j t) -> p j t", t=128),
                        ALU.mult)))(),
                        reads=(B("k_fm"), bS[2], bS[3]), writes=(B("k_fm"),), tag=f"kA2{mg}_{ttl}")
                    for j in range(NCT):
                        P.op("te", (lambda t=ttl, jj=j: (lambda e: e.transpose(
                            pC2.bitcast(BF16)[:, 128 * jj:128 * (jj + 1)],
                            k_fm[:, jj, 128 * t:128 * (t + 1)], C.s_idbf)))(),
                            reads=(B("k_fm"), C.b_ones), writes=(bC,), tag=f"ktr{mg}_{ttl}_{j}")
                    P.op("ve", (lambda t=ttl: (lambda e: e.tensor_copy(
                        ka_tm[:, t, :], pC2.bitcast(BF16)[:, 0:1024])))(),
                        reads=(bC,), writes=(B("ka_tm"),), tag=f"kte{mg}_{ttl}")
                    if C.debug and mg == 0 and ttl == 0:
                        C.dump("a1Ep0", Ep, (B("Ep"),))
                        C.dump("a1Em0", Em, (B("Em"),))
                if C.debug and mg == 0:
                    C.dump("a1ecols", ecols, (B("ecols"),))
                    C.dump("a1qtfm", q_fm, (B("q_fm"),))
                    C.dump("a1katfm", k_fm, (B("k_fm"),))
                    C.dump("a1katm", ka_tm, (B("ka_tm"),))
                    C.dump("a1xf_w", xf[:, sl], (B("xf"),))

        pOi_p = [pS[:, 0:512], pS[:, 512:1024]]
        bOi_p = [bS[0], bS[1]]
        pOa_h = [pS[:, 1024:1536], pO]
        bOa_h = [bS[2], bO]
        pKV_h = [pS[:, 1536:2048], pC2]
        bKV_h = [bS[3], bC]
        pPt = [pD, pE]
        bPt = [bD, bE]
        for cc in range(8):
            CP()
            ttl, half = cc // 2, cc % 2
            rb = 64 * half
            s1 = 4 * ttl + 2 * half
            e1v = ecols[:, :, s1:s1 + 1].broadcast_to((128, NCT, 64))
            e2v = ecols[:, :, s1 + 1:s1 + 2].broadcast_to((128, NCT, 64))
            pOa, bOa = pOa_h[half], bOa_h[half]
            pKV, bKV = pKV_h[half], bKV_h[half]
            P.op("ve", (lambda ev=e1v: (lambda e: e.tensor_tensor(S_all, S_all, ev, ALU.mult)))(),
                 reads=(B("S_all"), B("ecols")), writes=(B("S_all"),), tag=f"se1{mg}_{cc}")
            P.op("ve", lambda e: e.tensor_copy(S_bf, S_all),
                 reads=(B("S_all"),), writes=(B("S_bf"),), tag=f"sbf{mg}_{cc}")
            for j in range(NCT):
                for par in range(2):
                    P.op("te", (lambda jj=j, pr=par, t=ttl, r=rb, pq=pPt[par]: (lambda e: e.matmul(
                        pq[r:r + 64, 64 * jj:64 * (jj + 1)],
                        k_fm[64 * pr:64 * pr + 64, jj, 128 * t + r:128 * t + r + 64],
                        q_fm[64 * pr:64 * pr + 64, jj, 128 * t + r:128 * t + r + 64],
                        start=True, stop=True, tile_position=(64 * pr, r))))(),
                        reads=(B("k_fm"), B("q_fm")), writes=(bPt[par],), tag=f"pp{mg}_{cc}_{j}_{par}")
            for par in range(2):
                P.op("ve", (lambda pr=par, r=rb, pq=pPt[par]: (lambda e: e.tensor_tensor(
                    P_sb[r:r + 64, pr::2, :],
                    pq[r:r + 64, 0:512].rearrange("p (j t) -> p j t", t=64),
                    trim[r:r + 64, None, :].broadcast_to((64, NCT, 64)), ALU.mult)))(),
                    reads=(bPt[par], B("trim")), writes=(B("P_sb"),), tag=f"pe{mg}_{cc}_{par}")
            for j in range(NCT):
                for par in range(2):
                    h = 2 * j + par
                    P.op("te", (lambda jj=j, pr=par, c=cc, pq=pOi_p[par]: (lambda e: e.matmul(
                        pq[64 * pr:64 * pr + 64, 64 * jj:64 * (jj + 1)],
                        S_bf[64 * pr:64 * pr + 64, jj, :],
                        q_fm[64 * pr:64 * pr + 64, jj,
                             128 * (c // 2) + 64 * (c % 2):128 * (c // 2) + 64 * (c % 2) + 64],
                        start=True, stop=True, tile_position=(64 * pr, 64 * pr))))(),
                        reads=(B("S_bf"), B("q_fm")), writes=(bOi_p[par],), tag=f"oi{mg}_{cc}_{h}")
                    P.op("te", (lambda jj=j, pr=par, t=ttl, r=rb, hh=h, pq=pOa: (lambda e: e.matmul(
                        pq[64 * pr:64 * pr + 64, 64 * jj:64 * (jj + 1)],
                        v_tm[r:r + 64, t, 64 * hh:64 * (hh + 1)],
                        P_sb[r:r + 64, hh, :],
                        start=True, stop=True, tile_position=(r, 64 * pr))))(),
                        reads=(B("v_tm"), B("P_sb")), writes=(bOa,), tag=f"oa{mg}_{cc}_{h}")
                    P.op("te", (lambda jj=j, pr=par, t=ttl, r=rb, hh=h, pq=pKV: (lambda e: e.matmul(
                        pq[64 * pr:64 * pr + 64, 64 * jj:64 * (jj + 1)],
                        ka_tm[r:r + 64, t, 64 * hh:64 * (hh + 1)],
                        v_tm[r:r + 64, t, 64 * hh:64 * (hh + 1)],
                        start=True, stop=True, tile_position=(r, 64 * pr))))(),
                        reads=(B("ka_tm"), B("v_tm")), writes=(bKV,), tag=f"kv{mg}_{cc}_{h}")
            if C.debug and mg == 0 and cc == 1:
                C.dump("a1qscan_c1", q_fm, (B("q_fm"),))
                C.dump("a1sbfscan_c1", S_bf, (B("S_bf"),))
                C.dump("a1katm_c1", ka_tm, (B("ka_tm"),))
                C.dump("a1vtm_c1", v_tm, (B("v_tm"),))
            P.op("ve", (lambda pq=pKV: (lambda e: e.tensor_tensor(
                S_all, S_all, pq.rearrange("p (j t) -> p j t", t=64), ALU.add)))(),
                reads=(B("S_all"), bKV), writes=(B("S_all"),), tag=f"sa{mg}_{cc}")
            P.op("ve", (lambda ev=e2v: (lambda e: e.tensor_tensor(S_all, S_all, ev, ALU.mult)))(),
                 reads=(B("S_all"), B("ecols")), writes=(B("S_all"),), tag=f"se2{mg}_{cc}")
            for par in range(2):
                P.op("ve", (lambda pr=par, pi=pOi_p[par]: (lambda e: e.tensor_copy(
                    mbuf[64 * pr:64 * pr + 64, :], pi[64 * pr:64 * pr + 64, :])))(),
                    reads=(bOi_p[par],), writes=(B("mbuf"),), tag=f"oc{mg}_{cc}_{par}")
            if C.debug and mg == 0 and cc == 1:
                C.dump("a1sbf_c1", S_bf, (B("S_bf"),))
                C.dump("a1psb_c1", P_sb, (B("P_sb"),))
                C.dump("a1oi_c1", mbuf, (B("mbuf"),))
            P.op("ve", (lambda po=pOa: (lambda e: e.tensor_tensor(
                mbuf, mbuf, po, ALU.add)))(),
                reads=(B("mbuf"), bOa), writes=(B("mbuf"),), tag=f"ocs{mg}_{cc}")
            if C.debug and mg == 0 and cc == 1:
                C.dump("a1o_c1", mbuf, (B("mbuf"),))
            P.op("ve", (lambda t=ttl, r=rb: (lambda e: e.tensor_tensor(
                gate_fm[:, :, 128 * t + r:128 * t + r + 64],
                mbuf.rearrange("p (j t) -> p j t", t=64),
                gate_fm[:, :, 128 * t + r:128 * t + r + 64], ALU.mult)))(),
                reads=(B("mbuf"), B("gate_fm")), writes=(B("gate_fm"),), tag=f"og{mg}_{cc}")
            if C.debug and mg == 0 and cc == 0:
                C.dump("a1S_c0", S_all, (B("S_all"),))
        if C.debug and mg == 0:
            C.dump("a1ogfm", gate_fm, (B("gate_fm"),))
            C.dump("a1S_mg0", S_all, (B("S_all"),))

        CP()
        for kt in range(NCT):
            stream_w("Wo", kt, kt % 2, f"swo{mg}_{kt}")
            for ttl in range(4):
                for hf in range(2):
                    pj = pAll[2 * ttl + hf][:, 0:512]
                    bj = bAll[2 * ttl + hf]
                    P.op("te", (lambda t=ttl, k=kt, ss=kt % 2, h2=hf, pp=pj: (lambda e: e.matmul(
                        pp, gate_fm[:, k, 128 * t:128 * (t + 1)],
                        wstr[:, ss, 512 * h2:512 * (h2 + 1)],
                        start=(k == 0), stop=(k == NCT - 1))))(),
                        reads=(B("wstr"), B("gate_fm")), writes=(bj,), tag=f"wo{mg}_{kt}_{ttl}_{hf}")
        for ttl in range(4):
            tt = 4 * mg + ttl
            for hf in range(2):
                pj = pAll[2 * ttl + hf][:, 0:512]
                bj = bAll[2 * ttl + hf]
                P.op("ve", (lambda t=tt, h2=hf, pp=pj: (lambda e: e.tensor_tensor(
                    x_tm[:, t, 512 * h2:512 * (h2 + 1)], pp,
                    x_tm[:, t, 512 * h2:512 * (h2 + 1)], ALU.add)))(),
                    reads=(bj, b_xtm[tt]), writes=(b_xtm[tt],), tag=f"az{mg}_{ttl}_{hf}")
            P.op("sc", (lambda t=tt: (lambda e: e.activation(
                lzsq, x_tm[:, t, :], AF.Square, accum_out=lst[:, t, 0:1])))(),
                reads=(b_xtm[tt],), writes=(B("mbuf"), B("lst")), tag=f"asq{tt}")
            P.op("ve", (lambda t=tt: (lambda e: e.reduce_sum(
                lst[:, t, 1:2], x_tm[:, t, :], AXX)))(),
                reads=(b_xtm[tt],), writes=(B("lst"),), tag=f"asm{tt}")

    if C.debug:
        C.dump("a1zres", x_tm, tuple(b_xtm))
    ln_finish_and_apply(C, lst, lcols, B("lst"), B("lcols"), b_xtm, x_tm, "l1a")


def build_l1_ffn(C):
    nc, P = C.nc, C.P
    x_tm, xfm = C.x_tm, C.xfm
    b_xtm, b_xfm = C.b_xtm, C.b_xfm
    TC = 512
    NCH = T // TC

    bufs = {}
    def B(n):
        if n not in bufs:
            bufs[n] = Buf(n)
        return bufs[n]
    pS, pO, pC2, pD, pE = C.pS, C.pO, C.pC, C.pD, C.pE
    bS, bO, bC, bD, bE = C.bS, C.bO, C.bC, C.bD, C.bE
    pAll = [pS[:, 0:512], pS[:, 512:1024], pS[:, 1024:1536], pS[:, 1536:2048],
            pO, pC2, pD, pE]
    bAll = [bS[0], bS[1], bS[2], bS[3], bO, bC, bD, bE]

    C.arena_open()
    lst = C.at("f1_st", [128, NT, 2], F32)
    lcols = C.at("f1_cols", [128, NT, 3], F32)
    lzsq = C.at("f1_zsq", [128, D], BF16)
    mark = C.arena_mark()
    wtile = C.at("f1_wt", [128, 2, DH], F32)
    wtmp = C.at("f1_wm", [128, DH], F32)
    stile = C.at("f1_sg", [128, 2, DH], BF16)
    bwt = C.at("f1_bw", [128, 2, DH], BF16)
    acol = C.at("f1_ac", [128, 2, 1], F32)

    def bin_prep(wname, scbuf, b_sc, ntile, ncols, alpha_scale, tagp):
        for m in range(ntile):
            sl = m % 2
            P.op("sy", (lambda mm=m, ss=sl, nn=ncols: (lambda e: e.dma_start(
                out=wtile[:, ss, 0:nn], in_=C.win[wname][128 * mm:128 * (mm + 1), :])))(),
                reads=(C.b_win[wname],), writes=(B(f"wt{sl}"),), dma=True, tag=f"{tagp}l{m}")
            P.op("sc", (lambda ss=sl, nn=ncols: (lambda e: e.activation(
                wtmp[:, 0:nn], wtile[:, ss, 0:nn], AF.Abs,
                accum_out=acol[:, ss, 0:1])))(),
                reads=(B(f"wt{sl}"),), writes=(B("wtmp"), B(f"ac{sl}")), tag=f"{tagp}a{m}")
            P.op("sc", (lambda ss=sl, nn=ncols: (lambda e: e.activation(
                stile[:, ss, 0:nn], wtile[:, ss, 0:nn], AF.Sign)))(),
                reads=(B(f"wt{sl}"),), writes=(B(f"sg{sl}"),), tag=f"{tagp}s{m}")
            P.op("ve", (lambda ss=sl, nn=ncols, asc=alpha_scale: (lambda e: e.tensor_scalar(
                bwt[:, ss, 0:nn], stile[:, ss, 0:nn], acol[:, ss, 0:1], asc,
                ALU.mult, ALU.mult)))(),
                reads=(B(f"sg{sl}"), B(f"ac{sl}")), writes=(B(f"bw{sl}"),), tag=f"{tagp}q{m}")
            P.op("gp", (lambda mm=m, ss=sl, nn=ncols: (lambda e: e.dma_start(
                out=scbuf[128 * mm:128 * (mm + 1), :], in_=bwt[:, ss, 0:nn])))(),
                reads=(B(f"bw{sl}"),), writes=(b_sc,), dma=True, tag=f"{tagp}w{m}")

    bin_prep("L1_ffn_w1", C.sc_w1q, C.b_scw1q, 32, D, 1.0 / D, "f1p")
    bin_prep("L1_ffn_w2", C.sc_w2q, C.b_scw2q, 8, DH, 0.5 / DH, "f2p")

    C.arena_rewind(mark)
    h_fm = C.at("f1_h", [128, 32, TC], BF16)
    w1t = C.at("f1_w1t", [128, 2, 1024], BF16)
    w2t = C.at("f1_w2t", [128, 2, 1024], BF16)
    et = C.at("f1_et", [128, 2, TC], F32)
    brow1 = C.at("f1_b1", [1, DH], F32)
    brow2 = C.at("f1_b2", [1, D], F32)
    ones5 = C.at("f1_o5", [1, TC], F32)

    P.op("sy", lambda e: e.dma_start(out=brow1, in_=C.win["L1_ffn_b1"][None, :]),
         reads=(C.b_win["L1_ffn_b1"],), writes=(B("brow1"),), dma=True, tag="f1lb1")
    P.op("sy", lambda e: e.dma_start(out=brow2, in_=C.win["L1_ffn_b2"][None, :]),
         reads=(C.b_win["L1_ffn_b2"],), writes=(B("brow2"),), dma=True, tag="f1lb2")
    P.op("ve", lambda e: e.memset(ones5, 1.0), writes=(B("ones5"),), tag="f1o5")

    for ch in range(NCH):
        c0 = TC * ch
        for q in range(4):
            for kt in range(NCT):
                sl = kt % 2
                P.op("sy", (lambda qq=q, kk=kt, ss=sl: (lambda e: e.dma_start_transpose(
                    out=w1t[:, ss, :],
                    in_=C.sc_w1q[1024 * qq:1024 * (qq + 1), 128 * kk:128 * (kk + 1)])))(),
                    reads=(C.b_scw1q,), writes=(B(f"w1t{sl}"),), dma=True, tag=f"f1t{ch}_{q}_{kt}")
                for bk in range(8):
                    P.op("te", (lambda kk=kt, ss=sl, b2=bk, cc0=c0, pp=pAll[bk]: (lambda e: e.matmul(
                        pp[:, 0:TC], w1t[:, ss, 128 * b2:128 * (b2 + 1)],
                        xfm[:, kk, cc0:cc0 + TC],
                        start=(kk == 0), stop=False)))(),
                        reads=(B(f"w1t{sl}"), b_xfm[kt]), writes=(bAll[bk],),
                        tag=f"f1m{ch}_{q}_{kt}_{bk}")
            for bk in range(8):
                m = 8 * q + bk
                P.op("te", (lambda mm=m, pp=pAll[bk]: (lambda e: e.matmul(
                    pp[:, 0:TC], brow1[0:1, 128 * mm:128 * (mm + 1)], ones5,
                    start=False, stop=True)))(),
                    reads=(B("brow1"), B("ones5")), writes=(bAll[bk],), tag=f"f1b{ch}_{q}_{bk}")
            for bk in range(8):
                m = 8 * q + bk
                sl = bk % 2
                P.op("sc", (lambda ss=sl, pp=pAll[bk]: (lambda e: e.activation(
                    et[:, ss, :], pp[:, 0:TC], AF.Erf, scale=0.7071067811865476)))(),
                    reads=(bAll[bk],), writes=(B(f"et{sl}"),), tag=f"f1e{ch}_{q}_{bk}")
                P.op("ve", (lambda mm=m, ss=sl, pp=pAll[bk]: (lambda e: e.scalar_tensor_tensor(
                    h_fm[:, mm, :], et[:, ss, :], 1.0, pp[:, 0:TC],
                    ALU.add, ALU.mult)))(),
                    reads=(B(f"et{sl}"), bAll[bk]), writes=(B("h_fm"),), tag=f"f1g{ch}_{q}_{bk}")
        for kt2 in range(32):
            sl = kt2 % 2
            P.op("sc", (lambda kk=kt2, ss=sl: (lambda e: e.dma_start_transpose(
                out=w2t[:, ss, :],
                in_=C.sc_w2q[:, 128 * kk:128 * (kk + 1)])))(),
                reads=(C.b_scw2q,), writes=(B(f"w2t{sl}"),), dma=True, tag=f"f2t{ch}_{kt2}")
            for tt in range(4):
                for hf in range(2):
                    P.op("te", (lambda kk=kt2, ss=sl, t2=tt, h2=hf,
                                pp=pAll[2 * tt + hf]: (lambda e: e.matmul(
                        pp[:, 0:512], h_fm[:, kk, 128 * t2:128 * (t2 + 1)],
                        w2t[:, ss, 512 * h2:512 * (h2 + 1)],
                        start=(kk == 0), stop=False)))(),
                        reads=(B(f"w2t{sl}"), B("h_fm")), writes=(bAll[2 * tt + hf],),
                        tag=f"f2m{ch}_{kt2}_{tt}_{hf}")
        for tt in range(4):
            for hf in range(2):
                P.op("te", (lambda h2=hf, pp=pAll[2 * tt + hf]: (lambda e: e.matmul(
                    pp[:, 0:512], C.s_ones_f32, brow2[0:1, 512 * h2:512 * (h2 + 1)],
                    start=False, stop=True)))(),
                    reads=(B("brow2"), C.b_ones), writes=(bAll[2 * tt + hf],),
                    tag=f"f2b{ch}_{tt}_{hf}")
        for tt in range(4):
            tg = 4 * ch + tt
            for hf in range(2):
                P.op("ve", (lambda t2=tg, h2=hf, pp=pAll[2 * tt + hf]: (lambda e: e.tensor_tensor(
                    x_tm[:, t2, 512 * h2:512 * (h2 + 1)], pp[:, 0:512],
                    x_tm[:, t2, 512 * h2:512 * (h2 + 1)], ALU.add)))(),
                    reads=(bAll[2 * tt + hf], b_xtm[tg]), writes=(b_xtm[tg],),
                    tag=f"f2z{ch}_{tt}_{hf}")
            P.op("sc", (lambda t2=tg: (lambda e: e.activation(
                lzsq, x_tm[:, t2, :], AF.Square, accum_out=lst[:, t2, 0:1])))(),
                reads=(b_xtm[tg],), writes=(B("lzsq"), B("lst")), tag=f"f2q{tg}")
            P.op("ve", (lambda t2=tg: (lambda e: e.reduce_sum(
                lst[:, t2, 1:2], x_tm[:, t2, :], AXX)))(),
                reads=(b_xtm[tg],), writes=(B("lst"),), tag=f"f2s{tg}")

    ln_finish_and_apply(C, lst, lcols, B("lst"), B("lcols"), b_xtm, x_tm, "l1f")
    C.arena_close()



def build_nc(upto=4, debug=False):
    nc = bass.Bass("TRN2", target_bir_lowering=False, debug=False)
    C = Ctx(nc, upto, debug=debug)
    P = C.P

    C.dx = nc.dram_tensor("x", [T, D], F32, kind="ExternalInput").ap()
    C.dy = nc.dram_tensor("y", [T, D], F32, kind="ExternalOutput").ap()

    C.win = {}
    for name, shape in [
        ("L0_attn_w_q", [D, D]), ("L0_attn_w_k", [D, D]),
        ("L0_attn_w_v", [D, D]), ("L0_attn_w_o", [D, D]),
        ("L0_ffn_w1", [DH, D]), ("L0_ffn_conv_w", [DH, 3]), ("L0_ffn_conv_b", [DH]),
        ("L0_ffn_w2", [D, DH]),
        ("L1_attn_time_maa_x", [D]), ("L1_attn_time_maa_r", [D]),
        ("L1_attn_time_maa_k", [D]), ("L1_attn_time_maa_v", [D]),
        ("L1_attn_time_maa_w", [D]), ("L1_attn_time_maa_g", [D]),
        ("L1_attn_time_maa_w1", [D, 160]), ("L1_attn_time_maa_w2", [5, 32, D]),
        ("L1_attn_time_decay", [D]), ("L1_attn_time_decay_w1", [D, 64]),
        ("L1_attn_time_decay_w2", [64, D]),
        ("L1_attn_Wq", [D, D]), ("L1_attn_Wk", [D, D]), ("L1_attn_Wv", [D, D]),
        ("L1_attn_Wo", [D, D]), ("L1_attn_Wg", [D, D]), ("L1_attn_bg", [D]),
        ("L1_ffn_w1", [DH, D]), ("L1_ffn_b1", [DH]),
        ("L1_ffn_w2", [D, DH]), ("L1_ffn_b2", [D]),
    ]:
        C.win[name] = nc.dram_tensor(name, shape, F32, kind="ExternalInput").ap()
    C.b_win = {k: Buf("w_" + k, dram=True) for k in C.win}

    C.sc_x = nc.dram_tensor("sc_x", [T, D], BF16, kind="Internal").ap()
    C.b_scx = Buf("sc_x", dram=True)
    C.sc_mrow = nc.dram_tensor("sc_mrow", [16, 128], F32, kind="Internal").ap()
    C.b_scmd = Buf("sc_mrow", dram=True)
    C.sc_w = {}
    C.b_scw = {}
    for nm in ("wqT", "wkT", "wvT", "woT"):
        C.sc_w[nm] = nc.dram_tensor("sc_" + nm, [D, D], BF16, kind="Internal").ap()
        C.b_scw[nm] = Buf("sc_" + nm, dram=True)
    C.sc_w1q = nc.dram_tensor("sc_w1q", [DH, D], BF16, kind="Internal").ap()
    C.b_scw1q = Buf("sc_w1q", dram=True)
    C.sc_w2q = nc.dram_tensor("sc_w2q", [D, DH], BF16, kind="Internal").ap()
    C.b_scw2q = Buf("sc_w2q", dram=True)
    C.sc_l1w = {}
    C.b_scl1w = {}
    for nm in ("Wq", "Wk", "Wv", "Wg", "Wo"):
        C.sc_l1w[nm] = nc.dram_tensor("sc_l1" + nm, [D, D], BF16, kind="Internal").ap()
        C.b_scl1w[nm] = Buf("sc_l1" + nm, dram=True)

    C.x_tm = nc.alloc_sbuf_tensor("x_tm", [128, NT, D], F32).ap()
    C.xfm = nc.alloc_sbuf_tensor("xfm", [128, NCT, T], BF16).ap()
    C.s_ones_bf = nc.alloc_sbuf_tensor("s_ones_bf", [1, 128], BF16).ap()
    C.s_id64 = nc.alloc_sbuf_tensor("s_id64", [64, 64], BF16).ap()
    C.s_onescol = nc.alloc_sbuf_tensor("s_onescol", [128, 1], BF16).ap()
    C.s_e2 = nc.alloc_sbuf_tensor("s_e2", [128, 2], BF16).ap()
    C.s_onescol_f32 = nc.alloc_sbuf_tensor("s_onescol_f32", [128, 1], F32).ap()
    C.s_idf = nc.alloc_sbuf_tensor("s_idf", [128, 128], F32).ap()
    C.s_flsh = nc.alloc_sbuf_tensor("s_flsh", [128, 256], F32).ap()
    C.s_idbf = nc.alloc_sbuf_tensor("s_idbf", [128, 128], BF16).ap()
    C.s_sel = nc.alloc_sbuf_tensor("s_sel", [2, 256], F32).ap()
    C.s_ones_f32 = nc.alloc_sbuf_tensor("s_ones_f32", [1, 128], F32).ap()
    ones_bf = nc.inline_tensor(np.ones((1, 128), dtype=ml_dtypes.bfloat16), name="ones_bf").ap()
    ones_f32 = nc.inline_tensor(np.ones((1, 128), dtype=np.float32), name="ones_f32").ap()
    id64 = nc.inline_tensor(np.eye(64, dtype=ml_dtypes.bfloat16), name="id64").ap()
    onescol = nc.inline_tensor(np.ones((128, 1), dtype=ml_dtypes.bfloat16), name="onescol").ap()
    _e2 = np.zeros((128, 2), dtype=ml_dtypes.bfloat16); _e2[0:64, 0] = 1; _e2[64:128, 1] = 1
    e2c = nc.inline_tensor(_e2, name="e2c").ap()
    _sel = np.zeros((2, 256), dtype=np.float32); _sel[0, 0:128] = 1; _sel[1, 128:256] = 1
    selc = nc.inline_tensor(_sel, name="selc").ap()
    onescol_f32 = nc.inline_tensor(np.ones((128, 1), dtype=np.float32), name="onescol_f32").ap()
    idf = nc.inline_tensor(np.eye(128, dtype=np.float32), name="idf").ap()
    idbf = nc.inline_tensor(np.eye(128, dtype=ml_dtypes.bfloat16), name="idbf").ap()

    C.b_flsh = Buf("s_flsh")
    P.flush_buf = C.b_flsh
    P.flush_emit = (lambda e: e.memset(C.s_flsh, 0.0))
    C.b_xtm = [Buf(f"x_tm{t}") for t in range(NT)]
    C.b_xfm = [Buf(f"xfm{j}") for j in range(NCT)]
    C.b_ones = Buf("ones")
    C.b_dy = Buf("dy", dram=True)

    C.arena_reserve()

    C.pS = nc.alloc_psum_tensor("pS", [128, 2048], F32).ap()
    C.pO = nc.alloc_psum_tensor("pO", [128, 512], F32).ap()
    C.pC = nc.alloc_psum_tensor("pC", [128, 512], F32).ap()
    C.pD = nc.alloc_psum_tensor("pD", [128, 512], F32).ap()
    C.pE = nc.alloc_psum_tensor("pE", [128, 512], F32).ap()
    C.bS = [Buf(f"pS{i}") for i in range(4)]
    C.bO, C.bC, C.bD, C.bE = Buf("pO"), Buf("pC"), Buf("pD"), Buf("pE")

    P.op("sy", lambda e: e.dma_start(out=C.x_tm, in_=C.dx.rearrange("(tt p) d -> p tt d", p=128)),
         writes=tuple(C.b_xtm), dma=True, tag="ldx")
    P.op("sy", lambda e: e.dma_start(out=C.s_ones_bf, in_=ones_bf), writes=(C.b_ones,), dma=True, tag="ld1b")
    P.op("sy", lambda e: e.dma_start(out=C.s_ones_f32, in_=ones_f32), writes=(C.b_ones,), dma=True, tag="ld1f")
    P.op("sy", lambda e: e.dma_start(out=C.s_id64, in_=id64), writes=(C.b_ones,), dma=True, tag="ldid")
    P.op("sy", lambda e: e.dma_start(out=C.s_onescol, in_=onescol), writes=(C.b_ones,), dma=True, tag="ldoc")
    P.op("sy", lambda e: e.dma_start(out=C.s_e2, in_=e2c), writes=(C.b_ones,), dma=True, tag="lde2")
    P.op("sy", lambda e: e.dma_start(out=C.s_sel, in_=selc), writes=(C.b_ones,), dma=True, tag="ldsel")
    P.op("sy", lambda e: e.dma_start(out=C.s_onescol_f32, in_=onescol_f32), writes=(C.b_ones,), dma=True, tag="ldocf")
    P.op("sy", lambda e: e.dma_start(out=C.s_idf, in_=idf), writes=(C.b_ones,), dma=True, tag="ldidf")
    P.op("sy", lambda e: e.dma_start(out=C.s_idbf, in_=idbf), writes=(C.b_ones,), dma=True, tag="ldidb")

    for nm, src in [("wqT", "L0_attn_w_q"), ("wkT", "L0_attn_w_k"),
                    ("wvT", "L0_attn_w_v"), ("woT", "L0_attn_w_o")]:
        P.op("gp", (lambda s=C.win[src], t=C.sc_w[nm]: (lambda e: e.dma_start(out=t, in_=s)))(),
             reads=(C.b_win[src],), writes=(C.b_scw[nm],), dma=True, tag="cast_" + nm)

    def refresh_xfm(tag):
        P.op("gp", lambda e: e.dma_start(
            out=C.sc_x.rearrange("(tt p) d -> p tt d", p=128), in_=C.x_tm),
            reads=tuple(C.b_xtm), writes=(C.b_scx,), dma=True, tag="xc" + tag)
        for j in range(NCT):
            P.op("sy", (lambda jj=j: (lambda e: e.dma_start_transpose(
                out=C.xfm[:, jj, :], in_=C.sc_x[:, 128 * jj:128 * (jj + 1)])))(),
                reads=(C.b_scx,), writes=(C.b_xfm[j],), dma=True, tag=f"xT{tag}{j}")

    refresh_xfm("0")
    build_l0_attn(C)

    if upto >= 2:
        build_l0_ffn(C)
    if upto >= 3:
        refresh_xfm("2")
        build_l1_attn(C)
    if upto >= 4:
        refresh_xfm("3")
        build_l1_ffn(C)

    P.op("sy", lambda e: e.dma_start(
        out=C.dy.rearrange("(tt p) d -> p tt d", p=128), in_=C.x_tm),
        reads=tuple(C.b_xtm), writes=(C.b_dy,), dma=True, tag="sto")

    with nc.Block() as block:
        emit_for = P.finalize_emitters()

        @block.sync
        def _(sync):
            emit_for("sy", sync)
            P.final_waits(sync)

        @block.scalar
        def _(scalar):
            emit_for("sc", scalar)

        @block.vector
        def _(vector):
            emit_for("ve", vector)

        @block.gpsimd
        def _(gpsimd):
            emit_for("gp", gpsimd)

        @block.tensor
        def _(tensor):
            emit_for("te", tensor)

    P.close_sems()
    return nc



def _host_shape(k, v):
    a = np.asarray(v)
    if a.ndim == 3 and a.shape[0] == 1 and a.shape[1] == 1:
        return (a.shape[2],)
    if k == "L0_ffn_conv_w":
        return (DH, 3)
    return a.shape


def run_built(nc, inputs):
    xfull = np.ascontiguousarray(np.asarray(inputs["L0_attn_x"], dtype=np.float32))
    base = {k: np.ascontiguousarray(np.asarray(v, dtype=np.float32).reshape(_host_shape(k, v)))
            for k, v in inputs.items() if k != "L0_attn_x"}
    in_maps = []
    for b in range(8):
        m = dict(base)
        m["x"] = xfull[b]
        in_maps.append(m)
    res = run_bass_kernel_spmd(nc, in_maps, core_ids=list(range(8)))
    return np.stack([np.asarray(r["y"], dtype=np.float32) for r in res.results], axis=0)


def kernel(**inputs):
    nc = build_nc(upto=4)
    return run_built(nc, inputs)
